# version 29 (final)
# baseline (speedup 1.0000x reference)
import numpy as np
import ml_dtypes
import concourse.bass as bass
import concourse.mybir as mybir
from concourse.bass_utils import run_bass_kernel_spmd

F32 = mybir.dt.float32
BF16 = mybir.dt.bfloat16
AF = mybir.ActivationFunctionType
ALU = mybir.AluOpType
AX = mybir.AxisListType

H = 1024
NH = 16
NC = 8
TS = 122
WS = TS + 6
RW = 494
QW = 491
SCALE = 1.0 / 8.0

OWN = 4096
SHARD = OWN + 6
NT = 34
NP = 34


def set_geometry(own):
    global OWN, SHARD, NT, NP
    OWN = own
    SHARD = own + 6
    NT = (OWN + TS - 1) // TS
    NP = (SHARD + TS - 1) // TS


def tsz_of(i):
    return TS if i < NT - 1 else OWN - TS * (NT - 1)


def psz_of(p):
    return TS if p < NP - 1 else SHARD - TS * (NP - 1)


def wsz_of(i):
    return min(WS, SHARD - TS * i)


def build_program():
    nc = bass.Bass("TRN2", target_bir_lowering=False)
    x_h = nc.dram_tensor("x", [SHARD, H], F32, kind="ExternalInput").ap()
    wq_h = nc.dram_tensor("wq", [H, H], F32, kind="ExternalInput").ap()
    wk_h = nc.dram_tensor("wk", [H, H], F32, kind="ExternalInput").ap()
    wv_h = nc.dram_tensor("wv", [H, H], F32, kind="ExternalInput").ap()
    wo_h = nc.dram_tensor("wo", [H, H], F32, kind="ExternalInput").ap()
    wf_h = nc.dram_tensor("wf", [H, H], F32, kind="ExternalInput").ap()
    bq_h = nc.dram_tensor("bq", [4, 256], F32, kind="ExternalInput").ap()
    bv_h = nc.dram_tensor("bv", [128, 8], F32, kind="ExternalInput").ap()
    bo_h = nc.dram_tensor("bo", [1, H], F32, kind="ExternalInput").ap()
    mk_h = nc.dram_tensor("mk", [128, 3 * TS], BF16, kind="ExternalInput").ap()
    i4_h = nc.dram_tensor("ind4", [4, 488], BF16, kind="ExternalInput").ap()
    id_h = nc.dram_tensor("idm", [128, 128], BF16, kind="ExternalInput").ap()
    out_h = nc.dram_tensor("out", [OWN, H], F32, kind="ExternalOutput").ap()

    prog = {e: [] for e in ("sync", "gpsimd", "tensor", "vector", "scalar")}
    cnt = {"t": 0, "v": 0, "s": 0, "g": 0, "lx": 0, "lr": 0, "w1": 0, "w2": 0, "st": 0, "wg": 0, "w3": 0}
    evt = {}
    last_wait = {}

    def op(eng, f):
        last_wait[eng] = {}
        prog[eng].append(("op", f))

    def inc(eng, sem, f, key=None, n=1):
        cnt[sem] += n
        if key is not None:
            evt[key] = cnt[sem]
        last_wait[eng] = {}
        prog[eng].append(("inc", f, sem, n))

    def wait(eng, sem, val):
        if not val or val <= 0:
            return
        pend = last_wait.setdefault(eng, {})
        if pend.get(sem, 0) >= val:
            return
        pend[sem] = val
        prog[eng].append(("wait", sem, val))

    missed_waits = []

    def wait_ev(eng, sem, key):
        if key in evt:
            wait(eng, sem, evt[key])
        else:
            missed_waits.append(key)

    def dma(eng, chain, f, key=None):
        wait(eng, chain, cnt[chain])
        inc(eng, chain, f, key=key, n=16)

    def drn(eng):
        last_wait[eng] = {}
        prog[eng].append(("drain",))

    import contextlib
    with contextlib.ExitStack() as _st:
        wq_b = _st.enter_context(nc.sbuf_tensor("wq_b", [128, 8 * H], BF16))
        wk_b = _st.enter_context(nc.sbuf_tensor("wk_b", [128, 8 * H], BF16))
        wv_b = _st.enter_context(nc.sbuf_tensor("wv_b", [128, 8 * H], BF16))
        wo_b = _st.enter_context(nc.sbuf_tensor("wo_b", [128, 8 * H], BF16))
        wfT_b = _st.enter_context(nc.sbuf_tensor("wfT_b", [128, 8 * H], BF16))
        xnt = _st.enter_context(nc.sbuf_tensor("xnt", [128, 8 * RW], BF16))
        ktr = _st.enter_context(nc.sbuf_tensor("ktr", [128, 8 * RW], BF16))
        qme = _st.enter_context(nc.sbuf_tensor("qme", [128, 8 * QW], BF16))
        qmo = _st.enter_context(nc.sbuf_tensor("qmo", [128, 8 * QW], BF16))
        v_sb = _st.enter_context(nc.sbuf_tensor("v_sb", [128, 2 * NH * 65], BF16))
        ffn_sb = _st.enter_context(nc.sbuf_tensor("ffn_sb", [128, 2 * H], F32))
        ones128b = _st.enter_context(nc.sbuf_tensor("ones128b", [128, 1], BF16))
        e_buf = _st.enter_context(nc.sbuf_tensor("e_buf", [128, NH * TS], BF16))
        au_buf = _st.enter_context(nc.sbuf_tensor("au_buf", [128, NH * TS], BF16))
        attok = _st.enter_context(nc.sbuf_tensor("attok", [128, 2 * H], BF16))
        atT = _st.enter_context(nc.sbuf_tensor("atT", [128, 2 * 8 * TS], BF16))
        xl = _st.enter_context(nc.sbuf_tensor("xl", [128, 3 * H], F32))
        xr = _st.enter_context(nc.sbuf_tensor("xr", [128, 2 * H], F32))
        osb = _st.enter_context(nc.sbuf_tensor("osb", [128, 2 * H], F32))
        xn_sb = _st.enter_context(nc.sbuf_tensor("xn_sb", [128, 2 * H], BF16))
        scr2 = _st.enter_context(nc.sbuf_tensor("scr2", [128, H], BF16))
        gsc = _st.enter_context(nc.sbuf_tensor("gsc", [128, H], F32))
        wstage = _st.enter_context(nc.sbuf_tensor("wstage", [128, 4 * H], F32))
        wtern = _st.enter_context(nc.sbuf_tensor("wtern", [128, 2 * H], BF16))
        mk_sb = _st.enter_context(nc.sbuf_tensor("mk_sb", [128, 3 * TS], BF16))
        id_sb = _st.enter_context(nc.sbuf_tensor("id_sb", [128, 128], BF16))
        brow = _st.enter_context(nc.sbuf_tensor("brow", [1, H], BF16))
        bo_sb = _st.enter_context(nc.sbuf_tensor("bo_sb", [1, H], F32))
        bq_sb = _st.enter_context(nc.sbuf_tensor("bq_sb", [4, 256], F32))
        bqc_b = _st.enter_context(nc.sbuf_tensor("bqc_b", [4, 256], BF16))
        ind4_sb = _st.enter_context(nc.sbuf_tensor("ind4_sb", [4, 488], BF16))
        bv_sb = _st.enter_context(nc.sbuf_tensor("bv_sb", [128, 8], F32))
        bv_b = _st.enter_context(nc.sbuf_tensor("bv_b", [128, 8], BF16))
        ones_c = _st.enter_context(nc.sbuf_tensor("ones_c", [1, TS], BF16))
        ones128 = _st.enter_context(nc.sbuf_tensor("ones128", [128, 1], F32))
        onesrow = _st.enter_context(nc.sbuf_tensor("onesrow", [1, 128], F32))
        stats = _st.enter_context(nc.sbuf_tensor("stats", [128, 16], F32))
        recip = _st.enter_context(nc.sbuf_tensor("recip", [128, 16], F32))
        absacc = _st.enter_context(nc.sbuf_tensor("absacc", [128, 8], F32))
        sc_sc = _st.enter_context(nc.sbuf_tensor("sc_sc", [128, 4], F32))
        cst = _st.enter_context(nc.sbuf_tensor("cst", [128, 4], F32))
        p_tx = _st.enter_context(nc.psum_tensor("p_tx", [128, 1024], BF16))
        p_ta = _st.enter_context(nc.psum_tensor("p_ta", [128, 1024], BF16))
        p_qk = _st.enter_context(nc.psum_tensor("p_qk", [128, 1024], F32))
        p_of = _st.enter_context(nc.psum_tensor("p_of", [128, 1024], F32))
        p_sc = _st.enter_context(nc.psum_tensor("p_sc", [128, 1024], F32))
        S_T = _st.enter_context(nc.semaphore("s_t"))
        S_V = _st.enter_context(nc.semaphore("s_v"))
        S_S = _st.enter_context(nc.semaphore("s_s"))
        S_LX = _st.enter_context(nc.semaphore("s_lx"))
        S_LR = _st.enter_context(nc.semaphore("s_lr"))
        S_W1 = _st.enter_context(nc.semaphore("s_w1"))
        S_W2 = _st.enter_context(nc.semaphore("s_w2"))
        S_ST = _st.enter_context(nc.semaphore("s_st"))
        S_G = _st.enter_context(nc.semaphore("s_g"))
        S_WG = _st.enter_context(nc.semaphore("s_wg"))
        S_W3 = _st.enter_context(nc.semaphore("s_w3"))
        block = _st.enter_context(nc.Block())
        sems = {"t": S_T, "v": S_V, "s": S_S, "g": S_G, "lx": S_LX, "lr": S_LR, "w1": S_W1, "w2": S_W2, "st": S_ST, "wg": S_WG, "w3": S_W3}

        def xnt_ap():
            return xnt[:, :].rearrange("p (c n) -> p c n", c=8, n=RW)

        def ktr_ap():
            return ktr[:, :].rearrange("p (c n) -> p c n", c=8, n=RW)

        def qap(t):
            return t[:, :].rearrange("p (c n) -> p c n", c=8, n=QW)

        const_chains = ("w1", "wg", "w2", "w3", "lr", "lx")
        ld_pairs = []
        for ci, (nm, dst, src) in enumerate((("mk", mk_sb[:, :], mk_h[:, :]),
                                             ("id", id_sb[:, :], id_h[:, :]),
                                             ("bq", bq_sb[:, :], bq_h[:, :]),
                                             ("bv", bv_sb[:, :], bv_h[:, :]),
                                             ("bo", bo_sb[:, :], bo_h[:, :]),
                                             ("i4", ind4_sb[:, :], i4_h[:, :]))):
            ch = const_chains[ci]
            dma("sync", ch, (lambda d, s: (lambda e: e.dma_start(d, s)))(dst, src))
            ld_pairs.append((ch, cnt[ch]))

        for (ch, v0) in ld_pairs:
            wait("vector", ch, v0)
        op("vector", lambda e: e.memset(ones_c[0:1, :], 1.0))
        op("vector", lambda e: e.memset(ones128[:, :], 1.0))
        op("vector", lambda e: e.memset(onesrow[0:1, :], 1.0))
        op("vector", lambda e: e.memset(qme[64:128, :], 0.0))
        op("vector", lambda e: e.memset(qmo[0:64, :], 0.0))
        op("vector", lambda e: e.memset(ones128b[:, :], 1.0))
        op("vector", lambda e: e.memset(
            v_sb[:, :].rearrange("p (s h c) -> p s h c", s=2, h=NH, c=65)[:, :, :, 64:65], 1.0))
        op("vector", lambda e: e.memset(cst[:, 0:1], 1e-5))
        op("vector", lambda e: e.memset(cst[:, 1:2], -0.5))
        op("vector", lambda e: e.memset(cst[:, 2:3], 0.125))
        inc("vector", "v", lambda e: e.memset(stats[:, :], 0.0), key=("const",))
        for (ch, v0) in ld_pairs:
            wait("vector", ch, v0)
        inc("vector", "v", lambda e: e.tensor_copy(bv_b[:, :], bv_sb[:, :]), key=("bvb",))
        inc("vector", "v", lambda e: e.tensor_copy(bqc_b[:, :], bq_sb[:, :]), key=("bqb",))

        cast_done = []

        def emit_wcasts():
            cast_jobs = []
            for (srch, dstb) in ((wq_h, wq_b), (wk_h, wk_b), (wv_h, wv_b), (wo_h, wo_b)):
                for k in range(8):
                    cast_jobs.append((srch, dstb, k))
            chains4 = ("w1", "wg", "w2", "w3")
            cjob_done = []
            for j, (srch, dstb, k) in enumerate(cast_jobs):
                slot = j % 4
                ch = chains4[j % 4]
                if j >= 4:
                    pk, peng = cjob_done[j - 4]
                    wait("sync", {"vector": "v", "scalar": "s"}[peng], evt[pk])
                else:
                    wait("sync", "v", evt[("wtern", 4 + j)])
                dma("sync", ch,
                    (lambda sl, sh, kk: (lambda e: e.dma_start(
                        wstage[:, sl * H:(sl + 1) * H],
                        sh[128 * kk:128 * kk + 128, :])))(slot, srch, k),
                    key=("cwld", j))
                eng = "scalar" if (j % 2 == 0) else "vector"
                wait(eng, ch, evt[("cwld", j)])
                if eng == "scalar":
                    inc(eng, "s", (lambda sl, db, kk: (lambda e: e.copy(
                        db[:, kk * H:(kk + 1) * H], wstage[:, sl * H:(sl + 1) * H])))(slot, dstb, k),
                        key=("wcast", j))
                else:
                    inc(eng, "v", (lambda sl, db, kk: (lambda e: e.tensor_copy(
                        db[:, kk * H:(kk + 1) * H], wstage[:, sl * H:(sl + 1) * H])))(slot, dstb, k),
                        key=("wcast", j))
                cjob_done.append((("wcast", j), eng))
                cast_done.append((("wcast", j), eng))

        def emit_wf_prep():
            wjob_done = []
            jobs = [("wfabs", k) for k in range(4)] + [("wfp2", k) for k in range(4)]
            for j, (kind, k) in enumerate(jobs):
                slot = j % 2
                if j >= 2:
                    pk = wjob_done[j - 2]
                    wait("sync", "v", evt[pk])
                wchain = "w2" if (j % 2 == 0) else "w3"
                dma("sync", wchain,
                    (lambda sl, kk: (lambda e: e.dma_start(
                        wstage[:, 2 * sl * H:(2 * sl + 2) * H].rearrange("p (c n) -> p c n", c=2, n=H),
                        wf_h[256 * kk:256 * kk + 256, :].rearrange("(c p) n -> p c n", c=2, p=128))))(slot, k),
                    key=("wld", j))
                if kind == "wfabs":
                    wait("vector", wchain, evt[("wld", j)])
                    inc("vector", "v", (lambda sl, kk: (lambda e: e.tensor_reduce(
                        absacc[:, 2 * kk:2 * kk + 2],
                        wstage[:, 2 * sl * H:(2 * sl + 2) * H].rearrange("p (c n) -> p c n", c=2, n=H),
                        AX.X, ALU.add,
                        apply_absolute_value=True)))(slot, k), key=("wabs", j))
                    wjob_done.append(("wabs", j))
                    if k == 3:
                        drn("vector")
                        inc("vector", "v", lambda e: e.tensor_reduce(
                            stats[:, 15:16], absacc[:, 0:8], AX.X, ALU.add), key=("wabs_c",))
                else:
                    if k == 0:
                        wait("tensor", "v", evt[("wabs_c",)])
                        inc("tensor", "t", lambda e: e.matmul(
                            p_of[0:1, 8:9], ones128[:, 0:1], stats[:, 15:16], start=True, stop=True),
                            key=("abs_mm",))
                        wait("vector", "t", evt[("abs_mm",)])
                        inc("vector", "v", lambda e: e.tensor_scalar(
                            stats[0:1, 14:15], p_of[0:1, 8:9], 1.0 / (H * H), 1e-5,
                            ALU.mult, ALU.max), key=("scale1",))
                        drn("vector")
                        wait("tensor", "v", evt[("scale1",)])
                        inc("tensor", "t", lambda e: e.matmul(
                            p_of[0:128, 12:13], onesrow[0:1, 0:128], stats[0:1, 14:15],
                            start=True, stop=True), key=("scale_bc",))
                        wait("vector", "t", evt[("scale_bc",)])
                        inc("vector", "v", lambda e: e.tensor_scalar_mul(
                            sc_sc[:, 0:1], p_of[:, 12:13], 1.0), key=("scl0",))
                        inc("vector", "v", lambda e: e.tensor_scalar_mul(
                            sc_sc[:, 1:2], p_of[:, 12:13], 0.5), key=("scl1",))
                        inc("vector", "v", lambda e: e.tensor_scalar_mul(
                            sc_sc[:, 2:3], p_of[:, 12:13], -0.5), key=("scl2",))
                        inc("vector", "v", lambda e: e.tensor_scalar_mul(
                            sc_sc[:, 3:4], p_of[:, 12:13], -1.0), key=("scl3",))
                    wait("vector", wchain, evt[("wld", j)])
                    for kk2 in (("scl0",), ("scl1",), ("scl2",), ("scl3",)):
                        wait_ev("vector", "v", kk2)
                    for sub in range(2):
                        c = 2 * k + sub
                        wsl = c % 2
                        wait_ev("vector", "v", ("wtern", c - 2))
                        wait_ev("vector", "t", ("wfT_t", c - 2))
                        drn("vector")
                        op("vector", (lambda s2, sb: (lambda e: e.tensor_single_scalar(
                            gsc[:, 0:H], wstage[:, (2 * s2 + sb) * H:(2 * s2 + sb + 1) * H],
                            sc_sc[:, 1:2], ALU.is_ge)))(slot, sub))
                        drn("vector")
                        op("vector", (lambda s2, sb: (lambda e: e.scalar_tensor_tensor(
                            gsc[:, 0:H], wstage[:, (2 * s2 + sb) * H:(2 * s2 + sb + 1) * H],
                            sc_sc[:, 2:3], gsc[:, 0:H],
                            ALU.is_le, ALU.subtract)))(slot, sub))
                        drn("vector")
                        inc("vector", "v", (lambda w2: (lambda e: e.tensor_scalar_mul(
                            wtern[:, w2 * H:(w2 + 1) * H], gsc[:, 0:H], sc_sc[:, 3:4])))(wsl),
                            key=("wtern", c))
                        wait("tensor", "v", evt[("wtern", c)])
                        for (ch0, v0) in ld_pairs:
                            wait("tensor", ch0, v0)
                        if c > 0:
                            wait_ev("tensor", "s", ("wfT_ev", c - 1))
                        for ic in range(8):
                            f = (lambda ic2, w2: (lambda e: e.transpose(
                                p_ta[0:128, 128 * ic2:128 * ic2 + 128],
                                wtern[0:128, w2 * H + 128 * ic2:w2 * H + 128 * ic2 + 128],
                                id_sb[:, :])))(ic, wsl)
                            if ic < 7:
                                op("tensor", f)
                            else:
                                inc("tensor", "t", f, key=("wfT_t", c))
                        wait("scalar", "t", evt[("wfT_t", c)])
                        inc("scalar", "s", (lambda kk: (lambda e: e.copy(
                            wfT_b[:, :].rearrange("p (c n) -> p c n", c=8, n=H)[:, :, 128 * kk:128 * kk + 128],
                            p_ta[:, :].rearrange("p (c n) -> p c n", c=8, n=128))))(c),
                            key=("wfT_ev", c))
                    wjob_done.append(("wtern", 2 * k + 1))

        stash_P = {}
        stash_sc2 = {}
        stash_mid = {}

        def stage_XLD(p):
            if p >= 3:
                wait_ev("sync", "s", ("xn", p - 3))
                wait_ev("sync", "v", ("rsum", p - 3))
            dma("sync", "lx", (lambda pp, s2: (lambda e: e.dma_start(
                xl[0:psz_of(pp), s2 * H:(s2 + 1) * H],
                x_h[TS * pp:TS * pp + psz_of(pp), :])))(p, p % 3),
                key=("xld", p))

        def stage_LN_sq(p):
            wait("scalar", "lx", evt[("xld", p)])
            wait_ev("scalar", "v", ("const",))
            sqc = 8 + (p % 2)
            if p >= 2:
                wait_ev("scalar", "v", ("stats2", p - 2))
                wait_ev("scalar", "v", ("sm2", p - 2))
                wait_ev("scalar", "v", ("mbias", p - 2))
                wait_ev("scalar", "s", ("sq", p - 2))
                wait_ev("scalar", "s", ("xn", p - 2))
            if p >= 1:
                wait_ev("scalar", "s", ("sq", p - 1))
            inc("scalar", "s", (lambda pp, s2, c2: (lambda e: e.activation(
                scr2[0:psz_of(pp), 0:H], xl[0:psz_of(pp), s2 * H:(s2 + 1) * H], AF.Square,
                accum_out=stats[0:psz_of(pp), c2:c2 + 1])))(p, p % 3, sqc),
                key=("sq", p))

        def stage_LN_v(p):
            sc = 4 * (p % 2)
            sqc = 8 + (p % 2)
            wait("vector", "lx", evt[("xld", p)])
            wait_ev("vector", "v", ("const",))
            if p >= 2:
                wait_ev("vector", "s", ("xn", p - 2))
                wait_ev("vector", "v", ("mbias", p - 2))
                wait_ev("vector", "v", ("stats2", p - 2))
            inc("vector", "v", (lambda pp, s2, c: (lambda e: e.reduce_sum(
                stats[0:psz_of(pp), c:c + 1], xl[0:psz_of(pp), s2 * H:(s2 + 1) * H], AX.X)))(p, p % 3, sc),
                key=("rsum", p))
            wait_ev("vector", "s", ("sq", p))
            drn("vector")
            op("vector", (lambda pp, c: (lambda e: e.tensor_scalar_mul(
                stats[0:psz_of(pp), c + 1:c + 2], stats[0:psz_of(pp), c:c + 1], 1.0 / H)))(p, sc))
            inc("vector", "v", (lambda pp, c, c2: (lambda e: e.tensor_scalar_mul(
                stats[0:psz_of(pp), c + 2:c + 3], stats[0:psz_of(pp), c2:c2 + 1], 1.0 / H)))(p, sc, sqc),
                key=("sm2", p))
            drn("vector")
            op("vector", (lambda pp, c: (lambda e: e.tensor_tensor(
                stats[0:psz_of(pp), c + 3:c + 4], stats[0:psz_of(pp), c + 1:c + 2],
                stats[0:psz_of(pp), c + 1:c + 2], ALU.mult)))(p, sc))
            drn("vector")
            inc("vector", "v", (lambda pp, c: (lambda e: e.tensor_sub(
                stats[0:psz_of(pp), c + 2:c + 3], stats[0:psz_of(pp), c + 2:c + 3],
                stats[0:psz_of(pp), c + 3:c + 4])))(p, sc), key=("stats2", p))

        def stage_LN_mb(p):
            sc = 4 * (p % 2)
            wait_ev("vector", "s", ("rstd", p))
            drn("vector")
            inc("vector", "v", (lambda pp, c: (lambda e: e.tensor_scalar(
                stats[0:psz_of(pp), c + 1:c + 2], stats[0:psz_of(pp), c + 1:c + 2],
                stats[0:psz_of(pp), c + 3:c + 4], -1.0, ALU.mult, ALU.mult)))(p, sc),
                key=("mbias", p))

        def stage_LN_s(p):
            sc = 4 * (p % 2)
            wait_ev("scalar", "v", ("stats2", p))
            op("scalar", (lambda pp, c: (lambda e: e.activation(
                stats[0:psz_of(pp), c + 3:c + 4], stats[0:psz_of(pp), c + 2:c + 3], AF.Ln,
                bias=cst[0:psz_of(pp), 0:1])))(p, sc))
            drn("scalar")
            inc("scalar", "s", (lambda pp, c: (lambda e: e.activation(
                stats[0:psz_of(pp), c + 3:c + 4], stats[0:psz_of(pp), c + 3:c + 4], AF.Exp,
                scale=cst[0:psz_of(pp), 1:2])))(p, sc), key=("rstd", p))

        def stage_LN_xn(p):
            sc = 4 * (p % 2)
            sl = p % 2
            wait_ev("scalar", "v", ("mbias", p))
            wait_ev("scalar", "s", ("rstd", p))
            if p >= 2:
                wait_ev("scalar", "t", ("txn", p - 2))
            inc("scalar", "s", (lambda pp, s2, c, sl2: (lambda e: e.activation(
                xn_sb[0:psz_of(pp), sl2 * H:(sl2 + 1) * H], xl[0:psz_of(pp), s2 * H:(s2 + 1) * H],
                AF.Identity, bias=stats[0:psz_of(pp), c + 1:c + 2],
                scale=stats[0:psz_of(pp), c + 3:c + 4])))(p, p % 3, sc, sl),
                key=("xn", p))

        def stage_P_txn(p):
            psz = psz_of(p)
            rc = (TS * p) % 488
            sl = p % 2
            wait_ev("tensor", "s", ("xn", p))
            if p >= 1:
                wait_ev("tensor", "v", ("xnt_ev", p - 1))
            for c in range(8):
                f = (lambda pp, cc, sl2: (lambda e: e.transpose(
                    p_tx[0:128, TS * cc:TS * cc + psz_of(pp)],
                    xn_sb[0:psz_of(pp), sl2 * H + 128 * cc: sl2 * H + 128 * cc + 128],
                    id_sb[0:psz_of(pp), 0:psz_of(pp)])))(p, c, sl)
                if c < 7:
                    op("tensor", f)
                else:
                    inc("tensor", "t", f, key=("txn", p))
            wait_ev("vector", "t", ("txn", p))
            if p >= 4:
                wait_ev("vector", "t", ("ffn", p - 4))
                wait_ev("vector", "t", ("vwin", p - 4))
                wait_ev("vector", "v", ("xnt_ev", p - 4))
                wait_ev("vector", "v", ("xnt_ev2", p - 4))
            inc("vector", "v", (lambda pp, r2: (lambda e: e.tensor_copy(
                xnt_ap()[:, :, r2:r2 + psz_of(pp)],
                p_tx[:, 0:8 * TS].rearrange("p (c n) -> p c n", c=8, n=TS)[:, :, 0:psz_of(pp)])))(p, rc),
                key=("xnt_ev", p))
            if rc == 0:
                inc("vector", "v", (lambda pp: (lambda e: e.tensor_copy(
                    xnt_ap()[:, :, 488:494],
                    p_tx[:, 0:8 * TS].rearrange("p (c n) -> p c n", c=8, n=TS)[:, :, 0:6])))(p),
                    key=("xnt_ev2", p))

        def stage_P(p):
            psz = psz_of(p)
            rc = (TS * p) % 488
            wait_ev("tensor", "v", ("xnt_ev", p))
            if rc == 0:
                wait_ev("tensor", "v", ("xnt_ev2", p))
            [wait("tensor", c0, v0) for (c0, v0) in w_ready_pairs["wq"]]
            [wait("tensor", c0, v0) for (c0, v0) in w_ready_pairs["wk"]]
            if p >= 1:
                wait_ev("tensor", "s", ("kevB", p - 1))
                wait_ev("tensor", "v", ("qev_o", p - 1))
            if p >= 2:
                wait_ev("tensor", "s", ("ffnev", p - 2))
                wait_ev("tensor", "s", ("v65", p - 2))
            if p >= 4:
                wait_ev("tensor", "t", ("sc", p - 4, 1))

            def proj_bank(wb, bank, kkey):
                with_bias = wb is wq_b
                for mc in range(4 * bank, 4 * bank + 4):
                    col = 512 * bank + TS * (mc % 4)
                    for kc in range(8):
                        first = (mc % 4 == 0 and kc == 0)
                        last = (mc % 4 == 3 and kc == 7)
                        f = (lambda pp, m2, k2, c2, r2, wb2, st2, sp2: (lambda e: e.matmul(
                            p_qk[0:128, c2:c2 + psz_of(pp)],
                            wb2[0:128, k2 * H + 128 * m2: k2 * H + 128 * m2 + 128],
                            xnt_ap()[:, k2, r2:r2 + psz_of(pp)],
                            start=st2, stop=sp2)))(p, mc, kc, col, rc, wb,
                                                   first if with_bias else (kc == 0),
                                                   False if with_bias else (kc == 7))
                        if last and not with_bias:
                            inc("tensor", "t", f, key=kkey)
                        else:
                            op("tensor", f)
                if with_bias:
                    if psz == TS:
                        fb = (lambda pp, b2: (lambda e: e.matmul(
                            p_qk[0:128, 512 * b2:512 * b2 + 488],
                            bqc_b[0:4, 128 * b2:128 * b2 + 128],
                            ind4_sb[0:4, 0:488],
                            start=False, stop=True)))(p, bank)
                        inc("tensor", "t", fb, key=kkey)
                    else:
                        for mc4 in range(4):
                            fb = (lambda pp, b2, m4: (lambda e: e.matmul(
                                p_qk[0:128, 512 * b2 + TS * m4:512 * b2 + TS * m4 + psz_of(pp)],
                                bqc_b[0:4, 128 * b2:128 * b2 + 128],
                                ind4_sb[0:4, TS * m4:TS * m4 + psz_of(pp)],
                                start=False, stop=(m4 == 3))))(p, bank, mc4)
                            if mc4 == 3:
                                inc("tensor", "t", fb, key=kkey)
                            else:
                                op("tensor", fb)

            def evac_bank(kind, bank, srckey, outkey):
                wait_ev("scalar", "t", srckey)
                if kind == "q":
                    wait_ev("vector", "t", srckey)
                if p >= 4:
                    wait_ev("scalar", "s", (("qev_e", p - 4) if kind == "q" else ("kevB", p - 4)))
                    if kind == "q":
                        wait_ev("vector", "v", ("qev_o", p - 4))
                    wait_ev("scalar", "t", ("sc", p - 4, 1))
                    if kind == "q":
                        wait_ev("vector", "t", ("sc", p - 4, 1))
                    else:
                        wait_ev("scalar", "t", ("vwin", p - 4))
                dups = [(rc, psz_of(p))] + ([(488, 3 if kind == "q" else 6)] if rc == 0 else [])
                if kind == "q":
                    for di, (q2, w2) in enumerate(dups):
                        f_e = (lambda pp, b2, qq, ww: (lambda e: e.copy(
                            qap(qme)[0:64, 4 * b2:4 * b2 + 4, qq:qq + ww],
                            p_qk[0:64, :].rearrange("p (b n) -> p b n", b=2, n=512)[:, b2, 0:488]
                            .rearrange("p (c n) -> p c n", c=4, n=TS)[:, :, 0:ww])))(p, bank, q2, w2)
                        f_o = (lambda pp, b2, qq, ww: (lambda e: e.tensor_copy(
                            qap(qmo)[64:128, 4 * b2:4 * b2 + 4, qq:qq + ww],
                            p_qk[64:128, :].rearrange("p (b n) -> p b n", b=2, n=512)[:, b2, 0:488]
                            .rearrange("p (c n) -> p c n", c=4, n=TS)[:, :, 0:ww])))(p, bank, q2, w2)
                        last = (di == len(dups) - 1)
                        inc("scalar", "s", f_e, key=(outkey[0], p) if last else None)
                        inc("vector", "v", f_o, key=(outkey[1], p) if last else None)
                else:
                    for di, (r2, w2) in enumerate(dups):
                        f = (lambda pp, b2, rr, ww: (lambda e: e.copy(
                            ktr_ap()[:, 4 * b2:4 * b2 + 4, rr:rr + ww],
                            p_qk[:, :].rearrange("p (b n) -> p b n", b=2, n=512)[:, b2, 0:488]
                            .rearrange("p (c n) -> p c n", c=4, n=TS)[:, :, 0:ww])))(p, bank, r2, w2)
                        inc("scalar", "s", f, key=(outkey, p) if di == len(dups) - 1 else None)

            proj_bank(wq_b, 0, ("qpsA", p))
            if p >= 2:
                wait_ev("tensor", "v", ("recip", p - 2))
                wait_ev("tensor", "v", ("nrm", p - 2))
            proj_bank(wq_b, 1, ("qpsB", p))
            evac_bank("q", 0, ("qpsA", p), ("qev_eA", "qev_oA"))
            stash_P[p] = (proj_bank, evac_bank)

        def stage_P_rest(p):
            proj_bank, evac_bank = stash_P.pop(p)
            wait_ev("tensor", "s", ("qev_eA", p))
            wait_ev("tensor", "v", ("qev_oA", p))
            proj_bank(wk_b, 0, ("kpsA", p))
            evac_bank("q", 1, ("qpsB", p), ("qev_e", "qev_o"))
            evac_bank("k", 0, ("kpsA", p), "kevA")
            wait_ev("tensor", "s", ("qev_e", p))
            wait_ev("tensor", "v", ("qev_o", p))
            proj_bank(wk_b, 1, ("kpsB", p))
            evac_bank("k", 1, ("kpsB", p), "kevB")

        def stage_A_scores(i):
            tsz = tsz_of(i)
            wsz = wsz_of(i)
            rc = (TS * i) % 488
            qrc = (TS * i + 3) % 488
            vt = 0 if i == 0 else (2 if i == NT - 1 else 1)
            vsl = i % 2

            if i >= 2:
                wait_ev("sync", "g", ("out2", i - 2))
            dma("sync", "lr", (lambda ii, s2: (lambda e: e.dma_start(
                xr[0:tsz_of(ii), s2 * H:(s2 + 1) * H],
                x_h[3 + TS * ii: 3 + TS * ii + tsz_of(ii), :])))(i, i % 2),
                key=("xrld", i))

            def sc_half(half):
                if half == 1:
                    wait_ev("tensor", "s", ("exp", i, 0))
                    wait_ev("tensor", "s", ("kevB", min(i + 1, NP - 1)))
                else:
                    if i >= 1:
                        wait_ev("tensor", "v", ("nrm", i - 1))
                    wait_ev("tensor", "s", ("qev_e", min(i + 1, NP - 1)))
                    wait_ev("tensor", "v", ("qev_o", min(i + 1, NP - 1)))
                    wait_ev("tensor", "s", ("kevA", min(i + 1, NP - 1)))
                for hh in range(8):
                    h = half * 8 + hh
                    ch = h // 2
                    qring = qme if (h % 2 == 0) else qmo
                    col = 512 * (hh // 4) + TS * (hh % 4)
                    f = (lambda ii, c2, qr, cl, r2, q2: (lambda e: e.matmul(
                        p_sc[0:wsz_of(ii), cl:cl + tsz_of(ii)],
                        ktr_ap()[:, c2, r2:r2 + wsz_of(ii)],
                        qap(qr)[:, c2, q2:q2 + tsz_of(ii)],
                        start=True, stop=True)))(i, ch, qring, col, rc, qrc)
                    if hh == 7:
                        inc("tensor", "t", f, key=("sc", i, half))
                    else:
                        op("tensor", f)
                wait_ev("scalar", "t", ("sc", i, half))
                if i >= 1:
                    wait_ev("scalar", "v", ("au", i - 1, half))
                    wait_ev("scalar", "s", ("exp", i - 1, half))
                inc("scalar", "s", (lambda ii, hf: (lambda e: e.activation(
                    e_buf[0:wsz_of(ii), 8 * hf * TS:(8 * hf + 8) * TS]
                    .rearrange("p (a b n) -> p a b n", a=2, b=4, n=TS)[:, :, :, 0:tsz_of(ii)],
                    p_sc[0:wsz_of(ii), :].rearrange("p (b n) -> p b n", b=2, n=512)
                    [:, :, 0:488].rearrange("p b (g n) -> p b g n", g=4, n=TS)
                    [:, :, :, 0:tsz_of(ii)],
                    AF.Exp, scale=cst[0:wsz_of(ii), 2:3])))(i, hf=half), key=("exp", i, half))
                wait_ev("vector", "s", ("exp", i, half))
                if i >= 1:
                    wait_ev("vector", "t", ("attnV", i - 1))
                    wait_ev("vector", "v", ("au", i - 1, half))
                inc("vector", "v", (lambda ii, hf, v2: (lambda e: e.tensor_tensor(
                    au_buf[0:wsz_of(ii), :].rearrange("p (h n) -> p h n", h=NH, n=TS)
                    [:, 8 * hf:8 * hf + 8, 0:tsz_of(ii)],
                    e_buf[0:wsz_of(ii), :].rearrange("p (h n) -> p h n", h=NH, n=TS)
                    [:, 8 * hf:8 * hf + 8, 0:tsz_of(ii)],
                    mk_sb[0:wsz_of(ii), TS * v2:TS * v2 + tsz_of(ii)][:, None, :]
                    .broadcast_to([wsz_of(ii), 8, tsz_of(ii)]),
                    ALU.mult)))(i, half, vt), key=("au", i, half))

            sc_half(0)
            stash_mid[i] = True
            [wait("tensor", c0, v0) for (c0, v0) in w_ready_pairs["wv"]]
            if i >= 2:
                wait_ev("tensor", "t", ("attnV", i - 2))
            for jh in range(2):
                if jh == 0:
                    wait_ev("tensor", "s", ("kevA", min(i + 1, NP - 1)))
                    wait_ev("tensor", "v", ("qev_oA", min(i + 1, NP - 1)))
                else:
                    wait_ev("tensor", "s", ("kevB", min(i + 1, NP - 1)))
                    wait_ev("tensor", "v", ("qev_o", min(i + 1, NP - 1)))
                    if i >= 1:
                        wait_ev("tensor", "v", ("recip", i - 1))
                        wait_ev("tensor", "v", ("nrm", i - 1))
                for kc in range(8):
                    f = (lambda ii, k2, j2, r2: (lambda e: e.matmul(
                        p_qk[0:wsz_of(ii), 512 * j2:512 * j2 + 512],
                        xnt_ap()[:, k2, r2:r2 + wsz_of(ii)],
                        wv_b[0:128, k2 * H + 512 * j2: k2 * H + 512 * j2 + 512],
                        start=(k2 == 0), stop=(k2 == 7))))(i, kc, jh, rc)
                    if kc == 7 and jh == 1:
                        inc("tensor", "t", f, key=("vwin", i))
                    else:
                        op("tensor", f)
            wait_ev("scalar", "t", ("vwin", i))
            if i >= 2:
                wait_ev("scalar", "t", ("attnV", i - 2))
                wait_ev("scalar", "s", ("v65", i - 2))
            inc("scalar", "s", (lambda ii, s2: (lambda e: e.copy(
                v_sb[0:wsz_of(ii), s2 * NH * 65:(s2 + 1) * NH * 65]
                .rearrange("p (h c) -> p h c", h=NH, c=65)[:, :, 0:64],
                p_qk[0:wsz_of(ii), 0:H].rearrange("p (h c) -> p h c", h=NH, c=64))))(i, vsl),
                key=("v65", i))
            stash_sc2[i] = sc_half

        def stage_O_tat(i):
            tsz = tsz_of(i)
            osl = i % 2
            wait_ev("tensor", "v", ("nrm", i))
            if i == 0:
                wait_ev("tensor", "s", ("wfT_ev", 7))
            if i >= 1:
                wait_ev("tensor", "s", ("atT_ev", i - 1))
            for c in range(8):
                f = (lambda ii, cc, a2: (lambda e: e.transpose(
                    p_ta[0:128, TS * cc:TS * cc + tsz_of(ii)],
                    attok[0:tsz_of(ii), a2 * H + 128 * cc: a2 * H + 128 * cc + 128],
                    id_sb[0:tsz_of(ii), 0:tsz_of(ii)])))(i, c, osl)
                if c < 7:
                    op("tensor", f)
                else:
                    inc("tensor", "t", f, key=("tat", i))
            wait_ev("scalar", "t", ("tat", i))
            if i >= 2:
                wait_ev("scalar", "t", ("oproj", i - 2))
                wait_ev("scalar", "s", ("atT_ev", i - 2))
            inc("scalar", "s", (lambda ii, a2: (lambda e: e.copy(
                atT[:, :].rearrange("p (s c n) -> p s c n", s=2, c=8, n=TS)[:, a2, :, 0:tsz_of(ii)],
                p_ta[:, 0:8 * TS].rearrange("p (c n) -> p c n", c=8, n=TS)[:, :, 0:tsz_of(ii)])))(i, osl),
                key=("atT_ev", i))

        def stage_A_ffn(i):
            tsz = tsz_of(i)
            qrc = (TS * i + 3) % 488
            wait_ev("tensor", "s", ("v65", i))
            wait_ev("tensor", "s", ("wfT_ev", 7))
            if i >= 1:
                wait_ev("tensor", "v", ("recip", i - 1))
                wait_ev("tensor", "v", ("nrm", i - 1))
            for kc in range(8):
                for jh in range(2):
                    f = (lambda ii, k2, j2, q2: (lambda e: e.matmul(
                        p_qk[0:tsz_of(ii), 512 * j2:512 * j2 + 512],
                        xnt_ap()[:, k2, q2:q2 + tsz_of(ii)],
                        wfT_b[0:128, k2 * H + 512 * j2: k2 * H + 512 * j2 + 512],
                        start=(k2 == 0), stop=(k2 == 7))))(i, kc, jh, qrc)
                    if kc == 7 and jh == 1:
                        inc("tensor", "t", f, key=("ffn", i))
                    else:
                        op("tensor", f)
            wait_ev("scalar", "t", ("ffn", i))
            if i >= 2:
                wait_ev("scalar", "s", ("ffnev", i - 2))
                wait_ev("scalar", "v", ("out1", i - 2))
            inc("scalar", "s", (lambda ii, s2: (lambda e: e.copy(
                ffn_sb[0:tsz_of(ii), s2 * H:(s2 + 1) * H],
                p_qk[0:tsz_of(ii), 0:H])))(i, i % 2),
                key=("ffnev", i))

        def stage_O_proj(i):
            tsz = tsz_of(i)
            osl = i % 2
            if i >= 1:
                wait_ev("tensor", "v", ("out1", i - 1))
            if i == 0:
                wait_ev("tensor", "v", ("scl3",))
            wait_ev("tensor", "v", ("brow",))
            wait_ev("tensor", "s", ("atT_ev", i))
            [wait("tensor", c0, v0) for (c0, v0) in w_ready_pairs["wo"]]
            for jh in range(2):
                op("tensor", (lambda ii, j2: (lambda e: e.matmul(
                    p_of[0:tsz_of(ii), 512 * j2:512 * j2 + 512],
                    ones_c[0:1, 0:tsz_of(ii)], brow[0:1, 512 * j2:512 * j2 + 512],
                    start=True, stop=False)))(i, jh))
            for kc in range(8):
                for jh in range(2):
                    f = (lambda ii, k2, j2, a2: (lambda e: e.matmul(
                        p_of[0:tsz_of(ii), 512 * j2:512 * j2 + 512],
                        atT[:, :].rearrange("p (s c n) -> p s c n", s=2, c=8, n=TS)[:, a2, k2, 0:tsz_of(ii)],
                        wo_b[0:128, k2 * H + 512 * j2: k2 * H + 512 * j2 + 512],
                        start=False, stop=(k2 == 7))))(i, kc, jh, osl)
                    if kc == 7 and jh == 1:
                        inc("tensor", "t", f, key=("oproj", i))
                    else:
                        op("tensor", f)

        def stage_A_attnv(i):
            tsz = tsz_of(i)
            wsz = wsz_of(i)
            vsl = i % 2
            osl = i % 2
            wait_ev("tensor", "v", ("au", i, 0))
            wait_ev("tensor", "v", ("au", i, 1))
            wait_ev("tensor", "s", ("exp", i, 1))
            wait_ev("tensor", "s", ("v65", i))
            wait_ev("tensor", "s", ("ffnev", i))
            for h in range(NH):
                if h < 7:
                    c0 = 65 * h
                elif h < 14:
                    c0 = 512 + 65 * (h - 7)
                else:
                    c0 = 894 + 65 * (h - 14)
                dst = (p_sc if h < 14 else p_qk)[0:tsz_of(i), c0:c0 + 65]
                f = (lambda ii, h2, v2, d2: (lambda e: e.matmul(
                    d2,
                    au_buf[0:wsz_of(ii), TS * h2:TS * h2 + tsz_of(ii)],
                    v_sb[0:wsz_of(ii), v2 * NH * 65 + 65 * h2: v2 * NH * 65 + 65 * h2 + 65],
                    start=True, stop=True)))(i, h, vsl, dst)
                if h == NH - 1:
                    inc("tensor", "t", f, key=("attnV", i))
                else:
                    op("tensor", f)
            return

        def stage_Av_recip(i):
            wait_ev("vector", "t", ("attnV", i))
            if i >= 2:
                wait_ev("vector", "v", ("recip", i - 2))
            op("vector", (lambda ii: (lambda e: e.reciprocal(
                recip[0:tsz_of(ii), 0:7],
                p_sc[0:tsz_of(ii), 0:455].rearrange("p (h c) -> p h c", h=7, c=65)[:, :, 64])))(i))
            op("vector", (lambda ii: (lambda e: e.reciprocal(
                recip[0:tsz_of(ii), 7:14],
                p_sc[0:tsz_of(ii), 512:967].rearrange("p (h c) -> p h c", h=7, c=65)[:, :, 64])))(i))
            inc("vector", "v", (lambda ii: (lambda e: e.reciprocal(
                recip[0:tsz_of(ii), 14:16],
                p_qk[0:tsz_of(ii), 894:1024].rearrange("p (h c) -> p h c", h=2, c=65)[:, :, 64])))(i),
                key=("recip", i))

        def stage_Av_nrm(i):
            osl = i % 2
            if i >= 2:
                wait_ev("vector", "t", ("tat", i - 2))
                wait_ev("vector", "v", ("nrm", i - 2))
            drn("vector")
            op("vector", (lambda ii, a2: (lambda e: e.tensor_tensor(
                attok[0:tsz_of(ii), a2 * H:a2 * H + 448].rearrange("p (h c) -> p h c", h=7, c=64),
                p_sc[0:tsz_of(ii), 0:455].rearrange("p (h c) -> p h c", h=7, c=65)[:, :, 0:64],
                recip[0:tsz_of(ii), 0:7][:, :, None].broadcast_to([tsz_of(ii), 7, 64]),
                ALU.mult)))(i, osl))
            op("vector", (lambda ii, a2: (lambda e: e.tensor_tensor(
                attok[0:tsz_of(ii), a2 * H + 448:a2 * H + 896].rearrange("p (h c) -> p h c", h=7, c=64),
                p_sc[0:tsz_of(ii), 512:967].rearrange("p (h c) -> p h c", h=7, c=65)[:, :, 0:64],
                recip[0:tsz_of(ii), 7:14][:, :, None].broadcast_to([tsz_of(ii), 7, 64]),
                ALU.mult)))(i, osl))
            inc("vector", "v", (lambda ii, a2: (lambda e: e.tensor_tensor(
                attok[0:tsz_of(ii), a2 * H + 896:a2 * H + 1024].rearrange("p (h c) -> p h c", h=2, c=64),
                p_qk[0:tsz_of(ii), 894:1024].rearrange("p (h c) -> p h c", h=2, c=65)[:, :, 0:64],
                recip[0:tsz_of(ii), 14:16][:, :, None].broadcast_to([tsz_of(ii), 2, 64]),
                ALU.mult)))(i, osl), key=("nrm", i))

        def stage_O_res(i):
            tsz = tsz_of(i)
            osl = i % 2
            wait_ev("vector", "t", ("oproj", i))
            wait_ev("vector", "s", ("ffnev", i))
            if i >= 2:
                wait("vector", "st", 16 * (i - 1))
                wait_ev("vector", "g", ("out2", i - 2))
            inc("vector", "v", (lambda ii, o2: (lambda e: e.tensor_add(
                osb[0:tsz_of(ii), o2 * H:(o2 + 1) * H],
                p_of[0:tsz_of(ii), 0:H],
                ffn_sb[0:tsz_of(ii), (ii % 2) * H:((ii % 2) + 1) * H])))(i, osl), key=("out1", i))
            wait_ev("gpsimd", "v", ("out1", i))
            wait("gpsimd", "lr", evt[("xrld", i)])
            inc("gpsimd", "g", (lambda ii, s2, o2: (lambda e: e.tensor_add(
                osb[0:tsz_of(ii), o2 * H:(o2 + 1) * H],
                osb[0:tsz_of(ii), o2 * H:(o2 + 1) * H],
                xr[0:tsz_of(ii), s2 * H:(s2 + 1) * H])))(i, i % 2, osl), key=("out2", i))
            wait_ev("gpsimd", "g", ("out2", i))
            dma("gpsimd", "st", (lambda ii, o2: (lambda e: e.dma_start(
                out_h[TS * ii:TS * ii + tsz_of(ii), :],
                osb[0:tsz_of(ii), o2 * H:(o2 + 1) * H])))(i, osl))


        for p0 in range(min(3, NP)):
            stage_XLD(p0)
        for p0 in range(min(2, NP)):
            stage_LN_sq(p0)
            stage_LN_v(p0)
            stage_LN_s(p0)
            stage_LN_mb(p0)
            stage_LN_xn(p0)
        stage_P_txn(0)

        emit_wf_prep()
        emit_wcasts()
        w_ready_pairs = {}
        for nm, hi in (("wq", 7), ("wk", 15), ("wv", 23), ("wo", 31)):
            w_ready_pairs[nm] = (("s", evt[("wcast", hi - 1)]), ("v", evt[("wcast", hi)]))

        def emit_brow():
            [wait("tensor", c0, v0) for (c0, v0) in w_ready_pairs["wo"]]
            wait("tensor", "v", evt[("bvb",)])
            wait("tensor", "v", evt[("scl3",)])
            for k in range(8):
                for jh in range(2):
                    f = (lambda kk, j2: (lambda e: e.matmul(
                        p_of[0:1, 512 * j2:512 * j2 + 512], bv_b[:, kk:kk + 1],
                        wo_b[:, kk * H + 512 * j2: kk * H + 512 * j2 + 512],
                        start=(kk == 0), stop=(kk == 7))))(k, jh)
                    if k == 7 and jh == 1:
                        inc("tensor", "t", f, key=("bvwo",))
                    else:
                        op("tensor", f)
            wait("vector", "t", evt[("bvwo",)])
            inc("vector", "v", lambda e: e.tensor_add(brow[0:1, :], p_of[0:1, 0:H], bo_sb[0:1, :]),
                key=("brow",))




        for s in range(NP + 3):
            ia = s - 1
            io = s - 2
            ir = s - 3
            if s + 3 < NP:
                stage_XLD(s + 3)
            if 0 <= io < NT:
                stage_Av_recip(io)
                stage_Av_nrm(io)
            if s < NP:
                stage_P(s)
            if s < NP:
                stage_P_rest(s)
            if 0 <= ir < NT:
                stage_O_res(ir)
            if 0 <= ia < NT:
                stage_A_scores(ia)
            if s + 1 < NP:
                stage_P_txn(s + 1)
            if 0 <= io < NT:
                stage_O_tat(io)
            if s + 2 < NP:
                stage_LN_sq(s + 2)
                stage_LN_v(s + 2)
                stage_LN_s(s + 2)
                stage_LN_mb(s + 2)
                stage_LN_xn(s + 2)
            if 0 <= ia < NT:
                stash_sc2.pop(ia)(1)
                stage_A_ffn(ia)
            if s == 2:
                emit_brow()
            if 0 <= io < NT:
                stage_O_proj(io)
            if 0 <= ia < NT:
                stage_A_attnv(ia)

        wait("gpsimd", "st", 16 * NT)
        late = [k for k in missed_waits if k in evt]
        assert not late, f"wait_ev skipped keys registered later (emission-order bug): {sorted(set(late))[:8]}"


        def replay(e, entries):
            for ent in entries:
                if ent[0] == "op":
                    ent[1](e)
                elif ent[0] == "inc":
                    ent[1](e).then_inc(sems[ent[2]], ent[3])
                elif ent[0] == "drain":
                    e.drain()
                else:
                    e.wait_ge(sems[ent[1]], ent[2])

        @block.sync
        def _(sync):
            replay(sync, prog["sync"])

        @block.gpsimd
        def _(g):
            replay(g, prog["gpsimd"])

        @block.tensor
        def _(t):
            replay(t, prog["tensor"])

        @block.vector
        def _(v):
            replay(v, prog["vector"])

        @block.scalar
        def _(s):
            replay(s, prog["scalar"])

    return nc


def make_masks(core, n_cores=NC):
    jj = np.arange(128)[:, None]
    tl = np.arange(TS)[None, :]
    band = ((jj - tl >= 0) & (jj - tl <= 6)).astype(np.float32)
    m = np.stack([band.copy(), band.copy(), band.copy()])
    half = n_cores // 2
    if core % half == 0:
        m[0][:3, :] = 0.0
    tszl = tsz_of(NT - 1)
    wszl = wsz_of(NT - 1)
    m2 = band.copy()
    m2[wszl:, :] = 0.0
    m2[:, tszl:] = 0.0
    if core % half == half - 1:
        m2[tszl + 3:, :] = 0.0
    m[2] = m2
    return m.astype(ml_dtypes.bfloat16)


def kernel(L0_attn_hidden_states, L0_attn_w_q, L0_attn_b_q, L0_attn_w_k, L0_attn_b_k,
           L0_attn_w_v, L0_attn_b_v, L0_attn_w_o, L0_attn_b_o, L0_ffn_weight):
    x = np.ascontiguousarray(np.asarray(L0_attn_hidden_states, dtype=np.float32).reshape(-1, H))
    n_tok = x.shape[0]
    assert n_tok == NC * OWN
    xp = np.concatenate([np.zeros((3, H), np.float32), x, np.zeros((3, H), np.float32)], axis=0)

    wq = np.ascontiguousarray(L0_attn_w_q, dtype=np.float32)
    wk = np.ascontiguousarray(L0_attn_w_k, dtype=np.float32)
    wv = np.ascontiguousarray(L0_attn_w_v, dtype=np.float32)
    wo = np.ascontiguousarray(L0_attn_w_o, dtype=np.float32)
    wf = np.ascontiguousarray(L0_ffn_weight, dtype=np.float32)
    bqr = np.asarray(L0_attn_b_q, np.float32).reshape(8, 128)
    bq = np.ascontiguousarray(np.concatenate([bqr[0:4], bqr[4:8]], axis=1))
    ind4 = (np.arange(488)[None, :] // TS == np.arange(4)[:, None]).astype(ml_dtypes.bfloat16)
    bv = np.ascontiguousarray(np.asarray(L0_attn_b_v, np.float32).reshape(8, 128).T)
    bo = np.ascontiguousarray(np.asarray(L0_attn_b_o, np.float32).reshape(1, H))
    idm = np.eye(128).astype(ml_dtypes.bfloat16)

    nc = build_program()
    in_maps = []
    for c in range(NC):
        shard = np.ascontiguousarray(xp[c * OWN: c * OWN + SHARD])
        in_maps.append({
            "x": shard, "wq": wq, "wk": wk, "wv": wv, "wo": wo, "wf": wf,
            "bq": bq, "bv": bv, "bo": bo, "ind4": ind4,
            "mk": np.ascontiguousarray(make_masks(c).transpose(1, 0, 2).reshape(128, 3 * TS)),
            "idm": idm,
        })
    res = run_bass_kernel_spmd(nc, in_maps, core_ids=list(range(NC)))
    out = np.concatenate([r["out"] for r in res.results], axis=0)
    return out.reshape(2, n_tok // 2, H).astype(np.float32)


# version 30
# speedup vs baseline: 1.0013x; 1.0013x over previous
import numpy as np
import ml_dtypes
import concourse.bass as bass
import concourse.mybir as mybir
from concourse.bass_utils import run_bass_kernel_spmd

F32 = mybir.dt.float32
BF16 = mybir.dt.bfloat16
AF = mybir.ActivationFunctionType
ALU = mybir.AluOpType
AX = mybir.AxisListType

H = 1024
NH = 16
NC = 8
TS = 122
WS = TS + 6
RW = 494
QW = 491
SCALE = 1.0 / 8.0

OWN = 4096
SHARD = OWN + 6
NT = 34
NP = 34


def set_geometry(own):
    global OWN, SHARD, NT, NP
    OWN = own
    SHARD = own + 6
    NT = (OWN + TS - 1) // TS
    NP = (SHARD + TS - 1) // TS


def tsz_of(i):
    return TS if i < NT - 1 else OWN - TS * (NT - 1)


def psz_of(p):
    return TS if p < NP - 1 else SHARD - TS * (NP - 1)


def wsz_of(i):
    return min(WS, SHARD - TS * i)


def build_program():
    nc = bass.Bass("TRN2", target_bir_lowering=False)
    x_h = nc.dram_tensor("x", [SHARD, H], F32, kind="ExternalInput").ap()
    wq_h = nc.dram_tensor("wq", [H, H], F32, kind="ExternalInput").ap()
    wk_h = nc.dram_tensor("wk", [H, H], F32, kind="ExternalInput").ap()
    wv_h = nc.dram_tensor("wv", [H, H], F32, kind="ExternalInput").ap()
    wo_h = nc.dram_tensor("wo", [H, H], F32, kind="ExternalInput").ap()
    wf_h = nc.dram_tensor("wf", [H, H], F32, kind="ExternalInput").ap()
    bq_h = nc.dram_tensor("bq", [4, 256], F32, kind="ExternalInput").ap()
    bv_h = nc.dram_tensor("bv", [128, 8], F32, kind="ExternalInput").ap()
    bo_h = nc.dram_tensor("bo", [1, H], F32, kind="ExternalInput").ap()
    mk_h = nc.dram_tensor("mk", [128, 3 * TS], BF16, kind="ExternalInput").ap()
    i4_h = nc.dram_tensor("ind4", [4, 488], BF16, kind="ExternalInput").ap()
    id_h = nc.dram_tensor("idm", [128, 128], BF16, kind="ExternalInput").ap()
    out_h = nc.dram_tensor("out", [OWN, H], F32, kind="ExternalOutput").ap()

    prog = {e: [] for e in ("sync", "gpsimd", "tensor", "vector", "scalar")}
    cnt = {"t": 0, "v": 0, "s": 0, "g": 0, "lx": 0, "lr": 0, "w1": 0, "w2": 0, "st": 0, "wg": 0, "w3": 0}
    evt = {}
    last_wait = {}

    def op(eng, f):
        last_wait[eng] = {}
        prog[eng].append(("op", f))

    def inc(eng, sem, f, key=None, n=1):
        cnt[sem] += n
        if key is not None:
            evt[key] = cnt[sem]
        last_wait[eng] = {}
        prog[eng].append(("inc", f, sem, n))

    def wait(eng, sem, val):
        if not val or val <= 0:
            return
        pend = last_wait.setdefault(eng, {})
        if pend.get(sem, 0) >= val:
            return
        pend[sem] = val
        prog[eng].append(("wait", sem, val))

    missed_waits = []

    def wait_ev(eng, sem, key):
        if key in evt:
            wait(eng, sem, evt[key])
        else:
            missed_waits.append(key)

    def dma(eng, chain, f, key=None):
        wait(eng, chain, cnt[chain])
        inc(eng, chain, f, key=key, n=16)

    def drn(eng):
        last_wait[eng] = {}
        prog[eng].append(("drain",))

    import contextlib
    with contextlib.ExitStack() as _st:
        wq_b = _st.enter_context(nc.sbuf_tensor("wq_b", [128, 8 * H], BF16))
        wk_b = _st.enter_context(nc.sbuf_tensor("wk_b", [128, 8 * H], BF16))
        wv_b = _st.enter_context(nc.sbuf_tensor("wv_b", [128, 8 * H], BF16))
        wo_b = _st.enter_context(nc.sbuf_tensor("wo_b", [128, 8 * H], BF16))
        wfT_b = _st.enter_context(nc.sbuf_tensor("wfT_b", [128, 8 * H], BF16))
        xnt = _st.enter_context(nc.sbuf_tensor("xnt", [128, 8 * RW], BF16))
        ktr = _st.enter_context(nc.sbuf_tensor("ktr", [128, 8 * RW], BF16))
        qme = _st.enter_context(nc.sbuf_tensor("qme", [128, 8 * QW], BF16))
        qmo = _st.enter_context(nc.sbuf_tensor("qmo", [128, 8 * QW], BF16))
        v_sb = _st.enter_context(nc.sbuf_tensor("v_sb", [128, 2 * NH * 65], BF16))
        ffn_sb = _st.enter_context(nc.sbuf_tensor("ffn_sb", [128, 2 * H], F32))
        ones128b = _st.enter_context(nc.sbuf_tensor("ones128b", [128, 1], BF16))
        e_buf = _st.enter_context(nc.sbuf_tensor("e_buf", [128, NH * TS], BF16))
        au_buf = _st.enter_context(nc.sbuf_tensor("au_buf", [128, NH * TS], BF16))
        attok = _st.enter_context(nc.sbuf_tensor("attok", [128, 2 * H], BF16))
        atT = _st.enter_context(nc.sbuf_tensor("atT", [128, 2 * 8 * TS], BF16))
        xl = _st.enter_context(nc.sbuf_tensor("xl", [128, 3 * H], F32))
        xr = _st.enter_context(nc.sbuf_tensor("xr", [128, 2 * H], F32))
        osb = _st.enter_context(nc.sbuf_tensor("osb", [128, 2 * H], F32))
        xn_sb = _st.enter_context(nc.sbuf_tensor("xn_sb", [128, 2 * H], BF16))
        scr2 = _st.enter_context(nc.sbuf_tensor("scr2", [128, H], BF16))
        gsc = _st.enter_context(nc.sbuf_tensor("gsc", [128, H], F32))
        wstage = _st.enter_context(nc.sbuf_tensor("wstage", [128, 4 * H], F32))
        wtern = _st.enter_context(nc.sbuf_tensor("wtern", [128, 2 * H], BF16))
        mk_sb = _st.enter_context(nc.sbuf_tensor("mk_sb", [128, 3 * TS], BF16))
        id_sb = _st.enter_context(nc.sbuf_tensor("id_sb", [128, 128], BF16))
        brow = _st.enter_context(nc.sbuf_tensor("brow", [1, H], BF16))
        bo_sb = _st.enter_context(nc.sbuf_tensor("bo_sb", [1, H], F32))
        bq_sb = _st.enter_context(nc.sbuf_tensor("bq_sb", [4, 256], F32))
        bqc_b = _st.enter_context(nc.sbuf_tensor("bqc_b", [4, 256], BF16))
        ind4_sb = _st.enter_context(nc.sbuf_tensor("ind4_sb", [4, 488], BF16))
        bv_sb = _st.enter_context(nc.sbuf_tensor("bv_sb", [128, 8], F32))
        bv_b = _st.enter_context(nc.sbuf_tensor("bv_b", [128, 8], BF16))
        ones_c = _st.enter_context(nc.sbuf_tensor("ones_c", [1, TS], BF16))
        ones128 = _st.enter_context(nc.sbuf_tensor("ones128", [128, 1], F32))
        onesrow = _st.enter_context(nc.sbuf_tensor("onesrow", [1, 128], F32))
        stats = _st.enter_context(nc.sbuf_tensor("stats", [128, 16], F32))
        recip = _st.enter_context(nc.sbuf_tensor("recip", [128, 16], F32))
        absacc = _st.enter_context(nc.sbuf_tensor("absacc", [128, 8], F32))
        sc_sc = _st.enter_context(nc.sbuf_tensor("sc_sc", [128, 4], F32))
        cst = _st.enter_context(nc.sbuf_tensor("cst", [128, 4], F32))
        p_tx = _st.enter_context(nc.psum_tensor("p_tx", [128, 1024], BF16))
        p_ta = _st.enter_context(nc.psum_tensor("p_ta", [128, 1024], BF16))
        p_qk = _st.enter_context(nc.psum_tensor("p_qk", [128, 1024], F32))
        p_of = _st.enter_context(nc.psum_tensor("p_of", [128, 1024], F32))
        p_sc = _st.enter_context(nc.psum_tensor("p_sc", [128, 1024], F32))
        S_T = _st.enter_context(nc.semaphore("s_t"))
        S_V = _st.enter_context(nc.semaphore("s_v"))
        S_S = _st.enter_context(nc.semaphore("s_s"))
        S_LX = _st.enter_context(nc.semaphore("s_lx"))
        S_LR = _st.enter_context(nc.semaphore("s_lr"))
        S_W1 = _st.enter_context(nc.semaphore("s_w1"))
        S_W2 = _st.enter_context(nc.semaphore("s_w2"))
        S_ST = _st.enter_context(nc.semaphore("s_st"))
        S_G = _st.enter_context(nc.semaphore("s_g"))
        S_WG = _st.enter_context(nc.semaphore("s_wg"))
        S_W3 = _st.enter_context(nc.semaphore("s_w3"))
        block = _st.enter_context(nc.Block())
        sems = {"t": S_T, "v": S_V, "s": S_S, "g": S_G, "lx": S_LX, "lr": S_LR, "w1": S_W1, "w2": S_W2, "st": S_ST, "wg": S_WG, "w3": S_W3}

        def xnt_ap():
            return xnt[:, :].rearrange("p (c n) -> p c n", c=8, n=RW)

        def ktr_ap():
            return ktr[:, :].rearrange("p (c n) -> p c n", c=8, n=RW)

        def qap(t):
            return t[:, :].rearrange("p (c n) -> p c n", c=8, n=QW)

        const_chains = ("w1", "wg", "w2", "w3", "lr", "lx")
        ld_pairs = []
        for ci, (nm, dst, src) in enumerate((("mk", mk_sb[:, :], mk_h[:, :]),
                                             ("id", id_sb[:, :], id_h[:, :]),
                                             ("bq", bq_sb[:, :], bq_h[:, :]),
                                             ("bv", bv_sb[:, :], bv_h[:, :]),
                                             ("bo", bo_sb[:, :], bo_h[:, :]),
                                             ("i4", ind4_sb[:, :], i4_h[:, :]))):
            ch = const_chains[ci]
            dma("sync", ch, (lambda d, s: (lambda e: e.dma_start(d, s)))(dst, src))
            ld_pairs.append((ch, cnt[ch]))

        for (ch, v0) in ld_pairs:
            wait("vector", ch, v0)
        op("vector", lambda e: e.memset(ones_c[0:1, :], 1.0))
        op("vector", lambda e: e.memset(ones128[:, :], 1.0))
        op("vector", lambda e: e.memset(onesrow[0:1, :], 1.0))
        op("vector", lambda e: e.memset(qme[64:128, :], 0.0))
        op("vector", lambda e: e.memset(qmo[0:64, :], 0.0))
        op("vector", lambda e: e.memset(ones128b[:, :], 1.0))
        op("vector", lambda e: e.memset(
            v_sb[:, :].rearrange("p (s h c) -> p s h c", s=2, h=NH, c=65)[:, :, :, 64:65], 1.0))
        op("vector", lambda e: e.memset(cst[:, 0:1], 1e-5))
        op("vector", lambda e: e.memset(cst[:, 1:2], -0.5))
        op("vector", lambda e: e.memset(cst[:, 2:3], 0.125))
        inc("vector", "v", lambda e: e.memset(stats[:, :], 0.0), key=("const",))
        for (ch, v0) in ld_pairs:
            wait("vector", ch, v0)
        inc("vector", "v", lambda e: e.tensor_copy(bv_b[:, :], bv_sb[:, :]), key=("bvb",))
        inc("vector", "v", lambda e: e.tensor_copy(bqc_b[:, :], bq_sb[:, :]), key=("bqb",))

        cast_done = []

        def emit_wcasts():
            cast_jobs = []
            for (srch, dstb) in ((wq_h, wq_b), (wk_h, wk_b), (wv_h, wv_b), (wo_h, wo_b)):
                for k in range(8):
                    cast_jobs.append((srch, dstb, k))
            chains4 = ("w1", "wg", "w2", "w3")
            cjob_done = []
            for j, (srch, dstb, k) in enumerate(cast_jobs):
                slot = j % 4
                ch = chains4[j % 4]
                if j >= 4:
                    pk, peng = cjob_done[j - 4]
                    wait("sync", {"vector": "v", "scalar": "s"}[peng], evt[pk])
                else:
                    wait("sync", "v", evt[("wtern", 4 + j)])
                dma("sync", ch,
                    (lambda sl, sh, kk: (lambda e: e.dma_start(
                        wstage[:, sl * H:(sl + 1) * H],
                        sh[128 * kk:128 * kk + 128, :])))(slot, srch, k),
                    key=("cwld", j))
                eng = "scalar" if (j % 2 == 0) else "vector"
                wait(eng, ch, evt[("cwld", j)])
                if eng == "scalar":
                    inc(eng, "s", (lambda sl, db, kk: (lambda e: e.copy(
                        db[:, kk * H:(kk + 1) * H], wstage[:, sl * H:(sl + 1) * H])))(slot, dstb, k),
                        key=("wcast", j))
                else:
                    inc(eng, "v", (lambda sl, db, kk: (lambda e: e.tensor_copy(
                        db[:, kk * H:(kk + 1) * H], wstage[:, sl * H:(sl + 1) * H])))(slot, dstb, k),
                        key=("wcast", j))
                cjob_done.append((("wcast", j), eng))
                cast_done.append((("wcast", j), eng))

        def emit_wf_prep():
            wjob_done = []
            jobs = [("wfabs", k) for k in range(4)] + [("wfp2", k) for k in range(4)]
            for j, (kind, k) in enumerate(jobs):
                slot = j % 2
                if j >= 2:
                    pk = wjob_done[j - 2]
                    wait("sync", "v", evt[pk])
                wchain = "w2" if (j % 2 == 0) else "w3"
                dma("sync", wchain,
                    (lambda sl, kk: (lambda e: e.dma_start(
                        wstage[:, 2 * sl * H:(2 * sl + 2) * H].rearrange("p (c n) -> p c n", c=2, n=H),
                        wf_h[256 * kk:256 * kk + 256, :].rearrange("(c p) n -> p c n", c=2, p=128))))(slot, k),
                    key=("wld", j))
                if kind == "wfabs":
                    wait("vector", wchain, evt[("wld", j)])
                    inc("vector", "v", (lambda sl, kk: (lambda e: e.tensor_reduce(
                        absacc[:, 2 * kk:2 * kk + 2],
                        wstage[:, 2 * sl * H:(2 * sl + 2) * H].rearrange("p (c n) -> p c n", c=2, n=H),
                        AX.X, ALU.add,
                        apply_absolute_value=True)))(slot, k), key=("wabs", j))
                    wjob_done.append(("wabs", j))
                    if k == 3:
                        drn("vector")
                        inc("vector", "v", lambda e: e.tensor_reduce(
                            stats[:, 15:16], absacc[:, 0:8], AX.X, ALU.add), key=("wabs_c",))
                else:
                    if k == 0:
                        wait("tensor", "v", evt[("wabs_c",)])
                        inc("tensor", "t", lambda e: e.matmul(
                            p_of[0:1, 8:9], ones128[:, 0:1], stats[:, 15:16], start=True, stop=True),
                            key=("abs_mm",))
                        wait("vector", "t", evt[("abs_mm",)])
                        inc("vector", "v", lambda e: e.tensor_scalar(
                            stats[0:1, 14:15], p_of[0:1, 8:9], 1.0 / (H * H), 1e-5,
                            ALU.mult, ALU.max), key=("scale1",))
                        drn("vector")
                        wait("tensor", "v", evt[("scale1",)])
                        inc("tensor", "t", lambda e: e.matmul(
                            p_of[0:128, 12:13], onesrow[0:1, 0:128], stats[0:1, 14:15],
                            start=True, stop=True), key=("scale_bc",))
                        wait("vector", "t", evt[("scale_bc",)])
                        inc("vector", "v", lambda e: e.tensor_scalar_mul(
                            sc_sc[:, 0:1], p_of[:, 12:13], 1.0), key=("scl0",))
                        inc("vector", "v", lambda e: e.tensor_scalar_mul(
                            sc_sc[:, 1:2], p_of[:, 12:13], 0.5), key=("scl1",))
                        inc("vector", "v", lambda e: e.tensor_scalar_mul(
                            sc_sc[:, 2:3], p_of[:, 12:13], -0.5), key=("scl2",))
                        inc("vector", "v", lambda e: e.tensor_scalar_mul(
                            sc_sc[:, 3:4], p_of[:, 12:13], -1.0), key=("scl3",))
                    wait("vector", wchain, evt[("wld", j)])
                    for kk2 in (("scl0",), ("scl1",), ("scl2",), ("scl3",)):
                        wait_ev("vector", "v", kk2)
                    for sub in range(2):
                        c = 2 * k + sub
                        wsl = c % 2
                        wait_ev("vector", "v", ("wtern", c - 2))
                        wait_ev("vector", "t", ("wfT_t", c - 2))
                        drn("vector")
                        op("vector", (lambda s2, sb: (lambda e: e.tensor_single_scalar(
                            gsc[:, 0:H], wstage[:, (2 * s2 + sb) * H:(2 * s2 + sb + 1) * H],
                            sc_sc[:, 1:2], ALU.is_ge)))(slot, sub))
                        drn("vector")
                        op("vector", (lambda s2, sb: (lambda e: e.scalar_tensor_tensor(
                            gsc[:, 0:H], wstage[:, (2 * s2 + sb) * H:(2 * s2 + sb + 1) * H],
                            sc_sc[:, 2:3], gsc[:, 0:H],
                            ALU.is_le, ALU.subtract)))(slot, sub))
                        drn("vector")
                        inc("vector", "v", (lambda w2: (lambda e: e.tensor_scalar_mul(
                            wtern[:, w2 * H:(w2 + 1) * H], gsc[:, 0:H], sc_sc[:, 3:4])))(wsl),
                            key=("wtern", c))
                        wait("tensor", "v", evt[("wtern", c)])
                        for (ch0, v0) in ld_pairs:
                            wait("tensor", ch0, v0)
                        if c > 0:
                            wait_ev("tensor", "s", ("wfT_ev", c - 1))
                        for ic in range(8):
                            f = (lambda ic2, w2: (lambda e: e.transpose(
                                p_ta[0:128, 128 * ic2:128 * ic2 + 128],
                                wtern[0:128, w2 * H + 128 * ic2:w2 * H + 128 * ic2 + 128],
                                id_sb[:, :])))(ic, wsl)
                            if ic < 7:
                                op("tensor", f)
                            else:
                                inc("tensor", "t", f, key=("wfT_t", c))
                        wait("scalar", "t", evt[("wfT_t", c)])
                        inc("scalar", "s", (lambda kk: (lambda e: e.copy(
                            wfT_b[:, :].rearrange("p (c n) -> p c n", c=8, n=H)[:, :, 128 * kk:128 * kk + 128],
                            p_ta[:, :].rearrange("p (c n) -> p c n", c=8, n=128))))(c),
                            key=("wfT_ev", c))
                    wjob_done.append(("wtern", 2 * k + 1))

        stash_P = {}
        stash_sc2 = {}
        stash_mid = {}

        def stage_XLD(p):
            if p >= 3:
                wait_ev("sync", "s", ("xn", p - 3))
                wait_ev("sync", "v", ("rsum", p - 3))
            dma("sync", "lx", (lambda pp, s2: (lambda e: e.dma_start(
                xl[0:psz_of(pp), s2 * H:(s2 + 1) * H],
                x_h[TS * pp:TS * pp + psz_of(pp), :])))(p, p % 3),
                key=("xld", p))

        def stage_LN_sq(p):
            wait("scalar", "lx", evt[("xld", p)])
            wait_ev("scalar", "v", ("const",))
            sqc = 8 + (p % 2)
            if p >= 2:
                wait_ev("scalar", "v", ("stats2", p - 2))
                wait_ev("scalar", "v", ("sm2", p - 2))
                wait_ev("scalar", "v", ("mbias", p - 2))
                wait_ev("scalar", "s", ("sq", p - 2))
                wait_ev("scalar", "s", ("xn", p - 2))
            if p >= 1:
                wait_ev("scalar", "s", ("sq", p - 1))
            inc("scalar", "s", (lambda pp, s2, c2: (lambda e: e.activation(
                scr2[0:psz_of(pp), 0:H], xl[0:psz_of(pp), s2 * H:(s2 + 1) * H], AF.Square,
                accum_out=stats[0:psz_of(pp), c2:c2 + 1])))(p, p % 3, sqc),
                key=("sq", p))

        def stage_LN_v(p):
            sc = 4 * (p % 2)
            sqc = 8 + (p % 2)
            wait("vector", "lx", evt[("xld", p)])
            wait_ev("vector", "v", ("const",))
            if p >= 2:
                wait_ev("vector", "s", ("xn", p - 2))
                wait_ev("vector", "v", ("mbias", p - 2))
                wait_ev("vector", "v", ("stats2", p - 2))
            inc("vector", "v", (lambda pp, s2, c: (lambda e: e.reduce_sum(
                stats[0:psz_of(pp), c:c + 1], xl[0:psz_of(pp), s2 * H:(s2 + 1) * H], AX.X)))(p, p % 3, sc),
                key=("rsum", p))
            wait_ev("vector", "s", ("sq", p))
            drn("vector")
            op("vector", (lambda pp, c: (lambda e: e.tensor_scalar_mul(
                stats[0:psz_of(pp), c + 1:c + 2], stats[0:psz_of(pp), c:c + 1], 1.0 / H)))(p, sc))
            inc("vector", "v", (lambda pp, c, c2: (lambda e: e.tensor_scalar_mul(
                stats[0:psz_of(pp), c + 2:c + 3], stats[0:psz_of(pp), c2:c2 + 1], 1.0 / H)))(p, sc, sqc),
                key=("sm2", p))
            drn("vector")
            op("vector", (lambda pp, c: (lambda e: e.tensor_tensor(
                stats[0:psz_of(pp), c + 3:c + 4], stats[0:psz_of(pp), c + 1:c + 2],
                stats[0:psz_of(pp), c + 1:c + 2], ALU.mult)))(p, sc))
            drn("vector")
            inc("vector", "v", (lambda pp, c: (lambda e: e.tensor_sub(
                stats[0:psz_of(pp), c + 2:c + 3], stats[0:psz_of(pp), c + 2:c + 3],
                stats[0:psz_of(pp), c + 3:c + 4])))(p, sc), key=("stats2", p))

        def stage_LN_mb(p):
            sc = 4 * (p % 2)
            wait_ev("vector", "s", ("rstd", p))
            drn("vector")
            inc("vector", "v", (lambda pp, c: (lambda e: e.tensor_scalar(
                stats[0:psz_of(pp), c + 1:c + 2], stats[0:psz_of(pp), c + 1:c + 2],
                stats[0:psz_of(pp), c + 3:c + 4], -1.0, ALU.mult, ALU.mult)))(p, sc),
                key=("mbias", p))

        def stage_LN_s(p):
            sc = 4 * (p % 2)
            wait_ev("scalar", "v", ("stats2", p))
            op("scalar", (lambda pp, c: (lambda e: e.activation(
                stats[0:psz_of(pp), c + 3:c + 4], stats[0:psz_of(pp), c + 2:c + 3], AF.Ln,
                bias=cst[0:psz_of(pp), 0:1])))(p, sc))
            drn("scalar")
            inc("scalar", "s", (lambda pp, c: (lambda e: e.activation(
                stats[0:psz_of(pp), c + 3:c + 4], stats[0:psz_of(pp), c + 3:c + 4], AF.Exp,
                scale=cst[0:psz_of(pp), 1:2])))(p, sc), key=("rstd", p))

        def stage_LN_xn(p):
            sc = 4 * (p % 2)
            sl = p % 2
            wait_ev("scalar", "v", ("mbias", p))
            wait_ev("scalar", "s", ("rstd", p))
            if p >= 2:
                wait_ev("scalar", "t", ("txn", p - 2))
            inc("scalar", "s", (lambda pp, s2, c, sl2: (lambda e: e.activation(
                xn_sb[0:psz_of(pp), sl2 * H:(sl2 + 1) * H], xl[0:psz_of(pp), s2 * H:(s2 + 1) * H],
                AF.Identity, bias=stats[0:psz_of(pp), c + 1:c + 2],
                scale=stats[0:psz_of(pp), c + 3:c + 4])))(p, p % 3, sc, sl),
                key=("xn", p))

        def stage_P_txn(p):
            psz = psz_of(p)
            rc = (TS * p) % 488
            sl = p % 2
            wait_ev("tensor", "s", ("xn", p))
            if p >= 1:
                wait_ev("tensor", "v", ("xnt_ev", p - 1))
            for c in range(8):
                f = (lambda pp, cc, sl2: (lambda e: e.transpose(
                    p_tx[0:128, TS * cc:TS * cc + psz_of(pp)],
                    xn_sb[0:psz_of(pp), sl2 * H + 128 * cc: sl2 * H + 128 * cc + 128],
                    id_sb[0:psz_of(pp), 0:psz_of(pp)])))(p, c, sl)
                if c < 7:
                    op("tensor", f)
                else:
                    inc("tensor", "t", f, key=("txn", p))
            wait_ev("vector", "t", ("txn", p))
            if p >= 4:
                wait_ev("vector", "t", ("ffn", p - 4))
                wait_ev("vector", "t", ("vwin", p - 4))
                wait_ev("vector", "v", ("xnt_ev", p - 4))
                wait_ev("vector", "v", ("xnt_ev2", p - 4))
            inc("vector", "v", (lambda pp, r2: (lambda e: e.tensor_copy(
                xnt_ap()[:, :, r2:r2 + psz_of(pp)],
                p_tx[:, 0:8 * TS].rearrange("p (c n) -> p c n", c=8, n=TS)[:, :, 0:psz_of(pp)])))(p, rc),
                key=("xnt_ev", p))
            if rc == 0:
                inc("vector", "v", (lambda pp: (lambda e: e.tensor_copy(
                    xnt_ap()[:, :, 488:494],
                    p_tx[:, 0:8 * TS].rearrange("p (c n) -> p c n", c=8, n=TS)[:, :, 0:6])))(p),
                    key=("xnt_ev2", p))

        def stage_P(p):
            psz = psz_of(p)
            rc = (TS * p) % 488
            wait_ev("tensor", "v", ("xnt_ev", p))
            if rc == 0:
                wait_ev("tensor", "v", ("xnt_ev2", p))
            [wait("tensor", c0, v0) for (c0, v0) in w_ready_pairs["wq"]]
            [wait("tensor", c0, v0) for (c0, v0) in w_ready_pairs["wk"]]
            if p >= 1:
                wait_ev("tensor", "s", ("kevB", p - 1))
                wait_ev("tensor", "v", ("qev_o", p - 1))
            if p >= 2:
                wait_ev("tensor", "s", ("ffnev", p - 2))
                wait_ev("tensor", "s", ("v65", p - 2))
            if p >= 4:
                wait_ev("tensor", "t", ("sc", p - 4, 1))

            def proj_bank(wb, bank, kkey):
                with_bias = wb is wq_b
                for mc in range(4 * bank, 4 * bank + 4):
                    col = 512 * bank + TS * (mc % 4)
                    for kc in range(8):
                        first = (mc % 4 == 0 and kc == 0)
                        last = (mc % 4 == 3 and kc == 7)
                        f = (lambda pp, m2, k2, c2, r2, wb2, st2, sp2: (lambda e: e.matmul(
                            p_qk[0:128, c2:c2 + psz_of(pp)],
                            wb2[0:128, k2 * H + 128 * m2: k2 * H + 128 * m2 + 128],
                            xnt_ap()[:, k2, r2:r2 + psz_of(pp)],
                            start=st2, stop=sp2)))(p, mc, kc, col, rc, wb,
                                                   first if with_bias else (kc == 0),
                                                   False if with_bias else (kc == 7))
                        if last and not with_bias:
                            inc("tensor", "t", f, key=kkey)
                        else:
                            op("tensor", f)
                if with_bias:
                    if psz == TS:
                        fb = (lambda pp, b2: (lambda e: e.matmul(
                            p_qk[0:128, 512 * b2:512 * b2 + 488],
                            bqc_b[0:4, 128 * b2:128 * b2 + 128],
                            ind4_sb[0:4, 0:488],
                            start=False, stop=True)))(p, bank)
                        inc("tensor", "t", fb, key=kkey)
                    else:
                        for mc4 in range(4):
                            fb = (lambda pp, b2, m4: (lambda e: e.matmul(
                                p_qk[0:128, 512 * b2 + TS * m4:512 * b2 + TS * m4 + psz_of(pp)],
                                bqc_b[0:4, 128 * b2:128 * b2 + 128],
                                ind4_sb[0:4, TS * m4:TS * m4 + psz_of(pp)],
                                start=False, stop=(m4 == 3))))(p, bank, mc4)
                            if mc4 == 3:
                                inc("tensor", "t", fb, key=kkey)
                            else:
                                op("tensor", fb)

            def evac_bank(kind, bank, srckey, outkey):
                wait_ev("scalar", "t", srckey)
                if kind == "q":
                    wait_ev("vector", "t", srckey)
                if p >= 4:
                    wait_ev("scalar", "s", (("qev_e", p - 4) if kind == "q" else ("kevB", p - 4)))
                    if kind == "q":
                        wait_ev("vector", "v", ("qev_o", p - 4))
                    wait_ev("scalar", "t", ("sc", p - 4, 1))
                    if kind == "q":
                        wait_ev("vector", "t", ("sc", p - 4, 1))
                    else:
                        wait_ev("scalar", "t", ("vwin", p - 4))
                dups = [(rc, psz_of(p))] + ([(488, 3 if kind == "q" else 6)] if rc == 0 else [])
                if kind == "q":
                    for di, (q2, w2) in enumerate(dups):
                        f_e = (lambda pp, b2, qq, ww: (lambda e: e.copy(
                            qap(qme)[0:64, 4 * b2:4 * b2 + 4, qq:qq + ww],
                            p_qk[0:64, :].rearrange("p (b n) -> p b n", b=2, n=512)[:, b2, 0:488]
                            .rearrange("p (c n) -> p c n", c=4, n=TS)[:, :, 0:ww])))(p, bank, q2, w2)
                        f_o = (lambda pp, b2, qq, ww: (lambda e: e.tensor_copy(
                            qap(qmo)[64:128, 4 * b2:4 * b2 + 4, qq:qq + ww],
                            p_qk[64:128, :].rearrange("p (b n) -> p b n", b=2, n=512)[:, b2, 0:488]
                            .rearrange("p (c n) -> p c n", c=4, n=TS)[:, :, 0:ww])))(p, bank, q2, w2)
                        last = (di == len(dups) - 1)
                        inc("scalar", "s", f_e, key=(outkey[0], p) if last else None)
                        inc("vector", "v", f_o, key=(outkey[1], p) if last else None)
                else:
                    for di, (r2, w2) in enumerate(dups):
                        f = (lambda pp, b2, rr, ww: (lambda e: e.copy(
                            ktr_ap()[:, 4 * b2:4 * b2 + 4, rr:rr + ww],
                            p_qk[:, :].rearrange("p (b n) -> p b n", b=2, n=512)[:, b2, 0:488]
                            .rearrange("p (c n) -> p c n", c=4, n=TS)[:, :, 0:ww])))(p, bank, r2, w2)
                        inc("scalar", "s", f, key=(outkey, p) if di == len(dups) - 1 else None)

            proj_bank(wq_b, 0, ("qpsA", p))
            if p >= 2:
                wait_ev("tensor", "v", ("recip", p - 2))
                wait_ev("tensor", "v", ("nrm", p - 2))
            proj_bank(wq_b, 1, ("qpsB", p))
            evac_bank("q", 0, ("qpsA", p), ("qev_eA", "qev_oA"))
            stash_P[p] = (proj_bank, evac_bank)

        def stage_P_rest(p):
            proj_bank, evac_bank = stash_P.pop(p)
            wait_ev("tensor", "s", ("qev_eA", p))
            wait_ev("tensor", "v", ("qev_oA", p))
            proj_bank(wk_b, 0, ("kpsA", p))
            evac_bank("q", 1, ("qpsB", p), ("qev_e", "qev_o"))
            evac_bank("k", 0, ("kpsA", p), "kevA")
            wait_ev("tensor", "s", ("qev_e", p))
            wait_ev("tensor", "v", ("qev_o", p))
            proj_bank(wk_b, 1, ("kpsB", p))
            evac_bank("k", 1, ("kpsB", p), "kevB")

        def stage_A_scores(i):
            tsz = tsz_of(i)
            wsz = wsz_of(i)
            rc = (TS * i) % 488
            qrc = (TS * i + 3) % 488
            vt = 0 if i == 0 else (2 if i == NT - 1 else 1)
            vsl = i % 2

            if i >= 2:
                wait_ev("sync", "g", ("out2", i - 2))
            dma("sync", "lr", (lambda ii, s2: (lambda e: e.dma_start(
                xr[0:tsz_of(ii), s2 * H:(s2 + 1) * H],
                x_h[3 + TS * ii: 3 + TS * ii + tsz_of(ii), :])))(i, i % 2),
                key=("xrld", i))

            def sc_half(half):
                if half == 1:
                    wait_ev("tensor", "s", ("exp", i, 0))
                    wait_ev("tensor", "s", ("kevB", min(i + 1, NP - 1)))
                else:
                    if i >= 1:
                        wait_ev("tensor", "v", ("nrm", i - 1))
                    wait_ev("tensor", "s", ("qev_e", min(i + 1, NP - 1)))
                    wait_ev("tensor", "v", ("qev_o", min(i + 1, NP - 1)))
                    wait_ev("tensor", "s", ("kevA", min(i + 1, NP - 1)))
                for hh in range(8):
                    h = half * 8 + hh
                    ch = h // 2
                    qring = qme if (h % 2 == 0) else qmo
                    col = 512 * (hh // 4) + TS * (hh % 4)
                    f = (lambda ii, c2, qr, cl, r2, q2: (lambda e: e.matmul(
                        p_sc[0:wsz_of(ii), cl:cl + tsz_of(ii)],
                        ktr_ap()[:, c2, r2:r2 + wsz_of(ii)],
                        qap(qr)[:, c2, q2:q2 + tsz_of(ii)],
                        start=True, stop=True)))(i, ch, qring, col, rc, qrc)
                    if hh == 7:
                        inc("tensor", "t", f, key=("sc", i, half))
                    else:
                        op("tensor", f)
                wait_ev("scalar", "t", ("sc", i, half))
                if i >= 1:
                    wait_ev("scalar", "v", ("au", i - 1, half))
                    wait_ev("scalar", "s", ("exp", i - 1, half))
                inc("scalar", "s", (lambda ii, hf: (lambda e: e.activation(
                    e_buf[0:wsz_of(ii), 8 * hf * TS:(8 * hf + 8) * TS]
                    .rearrange("p (a b n) -> p a b n", a=2, b=4, n=TS)[:, :, :, 0:tsz_of(ii)],
                    p_sc[0:wsz_of(ii), :].rearrange("p (b n) -> p b n", b=2, n=512)
                    [:, :, 0:488].rearrange("p b (g n) -> p b g n", g=4, n=TS)
                    [:, :, :, 0:tsz_of(ii)],
                    AF.Exp, scale=cst[0:wsz_of(ii), 2:3])))(i, hf=half), key=("exp", i, half))
                wait_ev("vector", "s", ("exp", i, half))
                if i >= 1:
                    wait_ev("vector", "t", ("attnV", i - 1))
                    wait_ev("vector", "v", ("au", i - 1, half))
                inc("vector", "v", (lambda ii, hf, v2: (lambda e: e.tensor_tensor(
                    au_buf[0:wsz_of(ii), :].rearrange("p (h n) -> p h n", h=NH, n=TS)
                    [:, 8 * hf:8 * hf + 8, 0:tsz_of(ii)],
                    e_buf[0:wsz_of(ii), :].rearrange("p (h n) -> p h n", h=NH, n=TS)
                    [:, 8 * hf:8 * hf + 8, 0:tsz_of(ii)],
                    mk_sb[0:wsz_of(ii), TS * v2:TS * v2 + tsz_of(ii)][:, None, :]
                    .broadcast_to([wsz_of(ii), 8, tsz_of(ii)]),
                    ALU.mult)))(i, half, vt), key=("au", i, half))

            sc_half(0)
            stash_mid[i] = True
            [wait("tensor", c0, v0) for (c0, v0) in w_ready_pairs["wv"]]
            if i >= 2:
                wait_ev("tensor", "t", ("attnV", i - 2))
            for jh in range(2):
                if jh == 0:
                    wait_ev("tensor", "s", ("kevA", min(i + 1, NP - 1)))
                    wait_ev("tensor", "v", ("qev_oA", min(i + 1, NP - 1)))
                else:
                    wait_ev("tensor", "s", ("kevB", min(i + 1, NP - 1)))
                    wait_ev("tensor", "v", ("qev_o", min(i + 1, NP - 1)))
                    if i >= 1:
                        wait_ev("tensor", "v", ("recip", i - 1))
                        wait_ev("tensor", "v", ("nrm", i - 1))
                for kc in range(8):
                    f = (lambda ii, k2, j2, r2: (lambda e: e.matmul(
                        p_qk[0:wsz_of(ii), 512 * j2:512 * j2 + 512],
                        xnt_ap()[:, k2, r2:r2 + wsz_of(ii)],
                        wv_b[0:128, k2 * H + 512 * j2: k2 * H + 512 * j2 + 512],
                        start=(k2 == 0), stop=(k2 == 7))))(i, kc, jh, rc)
                    if kc == 7 and jh == 1:
                        inc("tensor", "t", f, key=("vwin", i))
                    else:
                        op("tensor", f)
            wait_ev("scalar", "t", ("vwin", i))
            if i >= 2:
                wait_ev("scalar", "t", ("attnV", i - 2))
                wait_ev("scalar", "s", ("v65", i - 2))
            inc("scalar", "s", (lambda ii, s2: (lambda e: e.copy(
                v_sb[0:wsz_of(ii), s2 * NH * 65:(s2 + 1) * NH * 65]
                .rearrange("p (h c) -> p h c", h=NH, c=65)[:, :, 0:64],
                p_qk[0:wsz_of(ii), 0:H].rearrange("p (h c) -> p h c", h=NH, c=64))))(i, vsl),
                key=("v65", i))
            stash_sc2[i] = sc_half

        def stage_O_tat(i):
            tsz = tsz_of(i)
            osl = i % 2
            wait_ev("tensor", "v", ("nrm", i))
            if i == 0:
                wait_ev("tensor", "s", ("wfT_ev", 7))
            if i >= 1:
                wait_ev("tensor", "s", ("atT_ev", i - 1))
            for c in range(8):
                f = (lambda ii, cc, a2: (lambda e: e.transpose(
                    p_ta[0:128, TS * cc:TS * cc + tsz_of(ii)],
                    attok[0:tsz_of(ii), a2 * H + 128 * cc: a2 * H + 128 * cc + 128],
                    id_sb[0:tsz_of(ii), 0:tsz_of(ii)])))(i, c, osl)
                if c < 7:
                    op("tensor", f)
                else:
                    inc("tensor", "t", f, key=("tat", i))
            wait_ev("scalar", "t", ("tat", i))
            if i >= 2:
                wait_ev("scalar", "t", ("oproj", i - 2))
                wait_ev("scalar", "s", ("atT_ev", i - 2))
            inc("scalar", "s", (lambda ii, a2: (lambda e: e.copy(
                atT[:, :].rearrange("p (s c n) -> p s c n", s=2, c=8, n=TS)[:, a2, :, 0:tsz_of(ii)],
                p_ta[:, 0:8 * TS].rearrange("p (c n) -> p c n", c=8, n=TS)[:, :, 0:tsz_of(ii)])))(i, osl),
                key=("atT_ev", i))

        def stage_A_ffn(i):
            tsz = tsz_of(i)
            qrc = (TS * i + 3) % 488
            wait_ev("tensor", "s", ("v65", i))
            wait_ev("tensor", "s", ("wfT_ev", 7))
            if i >= 1:
                wait_ev("tensor", "v", ("recip", i - 1))
                wait_ev("tensor", "v", ("nrm", i - 1))
            for kc in range(8):
                for jh in range(2):
                    f = (lambda ii, k2, j2, q2: (lambda e: e.matmul(
                        p_qk[0:tsz_of(ii), 512 * j2:512 * j2 + 512],
                        xnt_ap()[:, k2, q2:q2 + tsz_of(ii)],
                        wfT_b[0:128, k2 * H + 512 * j2: k2 * H + 512 * j2 + 512],
                        start=(k2 == 0), stop=(k2 == 7))))(i, kc, jh, qrc)
                    if kc == 7 and jh == 1:
                        inc("tensor", "t", f, key=("ffn", i))
                    else:
                        op("tensor", f)
            wait_ev("scalar", "t", ("ffn", i))
            if i >= 2:
                wait_ev("scalar", "s", ("ffnev", i - 2))
                wait_ev("scalar", "v", ("out1", i - 2))
            inc("scalar", "s", (lambda ii, s2: (lambda e: e.copy(
                ffn_sb[0:tsz_of(ii), s2 * H:(s2 + 1) * H],
                p_qk[0:tsz_of(ii), 0:H])))(i, i % 2),
                key=("ffnev", i))

        def stage_O_proj(i):
            tsz = tsz_of(i)
            osl = i % 2
            if i >= 1:
                wait_ev("tensor", "v", ("out1", i - 1))
            if i == 0:
                wait_ev("tensor", "v", ("scl3",))
            wait_ev("tensor", "v", ("brow",))
            wait_ev("tensor", "s", ("atT_ev", i))
            [wait("tensor", c0, v0) for (c0, v0) in w_ready_pairs["wo"]]
            for jh in range(2):
                op("tensor", (lambda ii, j2: (lambda e: e.matmul(
                    p_of[0:tsz_of(ii), 512 * j2:512 * j2 + 512],
                    ones_c[0:1, 0:tsz_of(ii)], brow[0:1, 512 * j2:512 * j2 + 512],
                    start=True, stop=False)))(i, jh))
            for kc in range(8):
                for jh in range(2):
                    f = (lambda ii, k2, j2, a2: (lambda e: e.matmul(
                        p_of[0:tsz_of(ii), 512 * j2:512 * j2 + 512],
                        atT[:, :].rearrange("p (s c n) -> p s c n", s=2, c=8, n=TS)[:, a2, k2, 0:tsz_of(ii)],
                        wo_b[0:128, k2 * H + 512 * j2: k2 * H + 512 * j2 + 512],
                        start=False, stop=(k2 == 7))))(i, kc, jh, osl)
                    if kc == 7 and jh == 1:
                        inc("tensor", "t", f, key=("oproj", i))
                    else:
                        op("tensor", f)

        def stage_A_attnv(i):
            tsz = tsz_of(i)
            wsz = wsz_of(i)
            vsl = i % 2
            osl = i % 2
            wait_ev("tensor", "v", ("au", i, 0))
            wait_ev("tensor", "v", ("au", i, 1))
            wait_ev("tensor", "s", ("exp", i, 1))
            wait_ev("tensor", "s", ("v65", i))
            wait_ev("tensor", "s", ("ffnev", i))
            for h in range(NH):
                if h < 7:
                    c0 = 65 * h
                elif h < 14:
                    c0 = 512 + 65 * (h - 7)
                else:
                    c0 = 894 + 65 * (h - 14)
                dst = (p_sc if h < 14 else p_qk)[0:tsz_of(i), c0:c0 + 65]
                f = (lambda ii, h2, v2, d2: (lambda e: e.matmul(
                    d2,
                    au_buf[0:wsz_of(ii), TS * h2:TS * h2 + tsz_of(ii)],
                    v_sb[0:wsz_of(ii), v2 * NH * 65 + 65 * h2: v2 * NH * 65 + 65 * h2 + 65],
                    start=True, stop=True)))(i, h, vsl, dst)
                if h == NH - 1:
                    inc("tensor", "t", f, key=("attnV", i))
                else:
                    op("tensor", f)
            return

        def stage_Av_recip(i):
            wait_ev("vector", "t", ("attnV", i))
            if i >= 2:
                wait_ev("vector", "v", ("recip", i - 2))
            op("vector", (lambda ii: (lambda e: e.reciprocal(
                recip[0:tsz_of(ii), 0:7],
                p_sc[0:tsz_of(ii), 0:455].rearrange("p (h c) -> p h c", h=7, c=65)[:, :, 64])))(i))
            op("vector", (lambda ii: (lambda e: e.reciprocal(
                recip[0:tsz_of(ii), 7:14],
                p_sc[0:tsz_of(ii), 512:967].rearrange("p (h c) -> p h c", h=7, c=65)[:, :, 64])))(i))
            inc("vector", "v", (lambda ii: (lambda e: e.reciprocal(
                recip[0:tsz_of(ii), 14:16],
                p_qk[0:tsz_of(ii), 894:1024].rearrange("p (h c) -> p h c", h=2, c=65)[:, :, 64])))(i),
                key=("recip", i))

        def stage_Av_nrm(i):
            osl = i % 2
            if i >= 2:
                wait_ev("vector", "t", ("tat", i - 2))
                wait_ev("vector", "v", ("nrm", i - 2))
            drn("vector")
            op("vector", (lambda ii, a2: (lambda e: e.tensor_tensor(
                attok[0:tsz_of(ii), a2 * H:a2 * H + 448].rearrange("p (h c) -> p h c", h=7, c=64),
                p_sc[0:tsz_of(ii), 0:455].rearrange("p (h c) -> p h c", h=7, c=65)[:, :, 0:64],
                recip[0:tsz_of(ii), 0:7][:, :, None].broadcast_to([tsz_of(ii), 7, 64]),
                ALU.mult)))(i, osl))
            op("vector", (lambda ii, a2: (lambda e: e.tensor_tensor(
                attok[0:tsz_of(ii), a2 * H + 448:a2 * H + 896].rearrange("p (h c) -> p h c", h=7, c=64),
                p_sc[0:tsz_of(ii), 512:967].rearrange("p (h c) -> p h c", h=7, c=65)[:, :, 0:64],
                recip[0:tsz_of(ii), 7:14][:, :, None].broadcast_to([tsz_of(ii), 7, 64]),
                ALU.mult)))(i, osl))
            inc("vector", "v", (lambda ii, a2: (lambda e: e.tensor_tensor(
                attok[0:tsz_of(ii), a2 * H + 896:a2 * H + 1024].rearrange("p (h c) -> p h c", h=2, c=64),
                p_qk[0:tsz_of(ii), 894:1024].rearrange("p (h c) -> p h c", h=2, c=65)[:, :, 0:64],
                recip[0:tsz_of(ii), 14:16][:, :, None].broadcast_to([tsz_of(ii), 2, 64]),
                ALU.mult)))(i, osl), key=("nrm", i))

        def stage_O_res(i):
            tsz = tsz_of(i)
            osl = i % 2
            wait_ev("vector", "t", ("oproj", i))
            wait_ev("vector", "s", ("ffnev", i))
            if i >= 2:
                wait("vector", "st", 16 * (i - 1))
                wait_ev("vector", "g", ("out2", i - 2))
            inc("vector", "v", (lambda ii, o2: (lambda e: e.tensor_add(
                osb[0:tsz_of(ii), o2 * H:(o2 + 1) * H],
                p_of[0:tsz_of(ii), 0:H],
                ffn_sb[0:tsz_of(ii), (ii % 2) * H:((ii % 2) + 1) * H])))(i, osl), key=("out1", i))
            wait_ev("gpsimd", "v", ("out1", i))
            wait("gpsimd", "lr", evt[("xrld", i)])
            inc("gpsimd", "g", (lambda ii, s2, o2: (lambda e: e.tensor_add(
                osb[0:tsz_of(ii), o2 * H:(o2 + 1) * H],
                osb[0:tsz_of(ii), o2 * H:(o2 + 1) * H],
                xr[0:tsz_of(ii), s2 * H:(s2 + 1) * H])))(i, i % 2, osl), key=("out2", i))
            wait_ev("gpsimd", "g", ("out2", i))
            dma("gpsimd", "st", (lambda ii, o2: (lambda e: e.dma_start(
                out_h[TS * ii:TS * ii + tsz_of(ii), :],
                osb[0:tsz_of(ii), o2 * H:(o2 + 1) * H])))(i, osl))


        for p0 in range(min(3, NP)):
            stage_XLD(p0)
        for p0 in range(min(2, NP)):
            stage_LN_sq(p0)
            stage_LN_v(p0)
            stage_LN_s(p0)
            stage_LN_mb(p0)
            stage_LN_xn(p0)
        for (ch0, v0) in ld_pairs:
            wait("tensor", ch0, v0)
        for _w in range(110):
            op("tensor", (lambda c2: (lambda e: e.transpose(
                p_ta[0:128, 128 * c2:128 * c2 + 128], id_sb[:, :], id_sb[:, :])))(_w % 8))

        stage_P_txn(0)

        emit_wf_prep()
        wait_ev("tensor", "s", ("wfT_ev", 7))
        for _w in range(140):
            op("tensor", (lambda c2: (lambda e: e.transpose(
                p_ta[0:128, 128 * c2:128 * c2 + 128], id_sb[:, :], id_sb[:, :])))(_w % 8))
        emit_wcasts()
        w_ready_pairs = {}
        for nm, hi in (("wq", 7), ("wk", 15), ("wv", 23), ("wo", 31)):
            w_ready_pairs[nm] = (("s", evt[("wcast", hi - 1)]), ("v", evt[("wcast", hi)]))

        def emit_brow():
            [wait("tensor", c0, v0) for (c0, v0) in w_ready_pairs["wo"]]
            wait("tensor", "v", evt[("bvb",)])
            wait("tensor", "v", evt[("scl3",)])
            for k in range(8):
                for jh in range(2):
                    f = (lambda kk, j2: (lambda e: e.matmul(
                        p_of[0:1, 512 * j2:512 * j2 + 512], bv_b[:, kk:kk + 1],
                        wo_b[:, kk * H + 512 * j2: kk * H + 512 * j2 + 512],
                        start=(kk == 0), stop=(kk == 7))))(k, jh)
                    if k == 7 and jh == 1:
                        inc("tensor", "t", f, key=("bvwo",))
                    else:
                        op("tensor", f)
            wait("vector", "t", evt[("bvwo",)])
            inc("vector", "v", lambda e: e.tensor_add(brow[0:1, :], p_of[0:1, 0:H], bo_sb[0:1, :]),
                key=("brow",))




        for s in range(NP + 3):
            ia = s - 1
            io = s - 2
            ir = s - 3
            if s + 3 < NP:
                stage_XLD(s + 3)
            if 0 <= io < NT:
                stage_Av_recip(io)
                stage_Av_nrm(io)
            if s < NP:
                stage_P(s)
            if s < NP:
                stage_P_rest(s)
            if 0 <= ir < NT:
                stage_O_res(ir)
            if 0 <= ia < NT:
                stage_A_scores(ia)
            if s + 1 < NP:
                stage_P_txn(s + 1)
            if 0 <= io < NT:
                stage_O_tat(io)
            if s + 2 < NP:
                stage_LN_sq(s + 2)
                stage_LN_v(s + 2)
                stage_LN_s(s + 2)
                stage_LN_mb(s + 2)
                stage_LN_xn(s + 2)
            if 0 <= ia < NT:
                stash_sc2.pop(ia)(1)
                stage_A_ffn(ia)
            if s == 2:
                emit_brow()
            if 0 <= io < NT:
                stage_O_proj(io)
            if 0 <= ia < NT:
                stage_A_attnv(ia)

        wait("gpsimd", "st", 16 * NT)
        late = [k for k in missed_waits if k in evt]
        assert not late, f"wait_ev skipped keys registered later (emission-order bug): {sorted(set(late))[:8]}"


        def replay(e, entries):
            for ent in entries:
                if ent[0] == "op":
                    ent[1](e)
                elif ent[0] == "inc":
                    ent[1](e).then_inc(sems[ent[2]], ent[3])
                elif ent[0] == "drain":
                    e.drain()
                else:
                    e.wait_ge(sems[ent[1]], ent[2])

        @block.sync
        def _(sync):
            replay(sync, prog["sync"])

        @block.gpsimd
        def _(g):
            replay(g, prog["gpsimd"])

        @block.tensor
        def _(t):
            replay(t, prog["tensor"])

        @block.vector
        def _(v):
            replay(v, prog["vector"])

        @block.scalar
        def _(s):
            replay(s, prog["scalar"])

    return nc


def make_masks(core, n_cores=NC):
    jj = np.arange(128)[:, None]
    tl = np.arange(TS)[None, :]
    band = ((jj - tl >= 0) & (jj - tl <= 6)).astype(np.float32)
    m = np.stack([band.copy(), band.copy(), band.copy()])
    half = n_cores // 2
    if core % half == 0:
        m[0][:3, :] = 0.0
    tszl = tsz_of(NT - 1)
    wszl = wsz_of(NT - 1)
    m2 = band.copy()
    m2[wszl:, :] = 0.0
    m2[:, tszl:] = 0.0
    if core % half == half - 1:
        m2[tszl + 3:, :] = 0.0
    m[2] = m2
    return m.astype(ml_dtypes.bfloat16)


def kernel(L0_attn_hidden_states, L0_attn_w_q, L0_attn_b_q, L0_attn_w_k, L0_attn_b_k,
           L0_attn_w_v, L0_attn_b_v, L0_attn_w_o, L0_attn_b_o, L0_ffn_weight):
    x = np.ascontiguousarray(np.asarray(L0_attn_hidden_states, dtype=np.float32).reshape(-1, H))
    n_tok = x.shape[0]
    assert n_tok == NC * OWN
    xp = np.concatenate([np.zeros((3, H), np.float32), x, np.zeros((3, H), np.float32)], axis=0)

    wq = np.ascontiguousarray(L0_attn_w_q, dtype=np.float32)
    wk = np.ascontiguousarray(L0_attn_w_k, dtype=np.float32)
    wv = np.ascontiguousarray(L0_attn_w_v, dtype=np.float32)
    wo = np.ascontiguousarray(L0_attn_w_o, dtype=np.float32)
    wf = np.ascontiguousarray(L0_ffn_weight, dtype=np.float32)
    bqr = np.asarray(L0_attn_b_q, np.float32).reshape(8, 128)
    bq = np.ascontiguousarray(np.concatenate([bqr[0:4], bqr[4:8]], axis=1))
    ind4 = (np.arange(488)[None, :] // TS == np.arange(4)[:, None]).astype(ml_dtypes.bfloat16)
    bv = np.ascontiguousarray(np.asarray(L0_attn_b_v, np.float32).reshape(8, 128).T)
    bo = np.ascontiguousarray(np.asarray(L0_attn_b_o, np.float32).reshape(1, H))
    idm = np.eye(128).astype(ml_dtypes.bfloat16)

    nc = build_program()
    in_maps = []
    for c in range(NC):
        shard = np.ascontiguousarray(xp[c * OWN: c * OWN + SHARD])
        in_maps.append({
            "x": shard, "wq": wq, "wk": wk, "wv": wv, "wo": wo, "wf": wf,
            "bq": bq, "bv": bv, "bo": bo, "ind4": ind4,
            "mk": np.ascontiguousarray(make_masks(c).transpose(1, 0, 2).reshape(128, 3 * TS)),
            "idm": idm,
        })
    res = run_bass_kernel_spmd(nc, in_maps, core_ids=list(range(NC)))
    out = np.concatenate([r["out"] for r in res.results], axis=0)
    return out.reshape(2, n_tok // 2, H).astype(np.float32)


# version 31
# speedup vs baseline: 1.0103x; 1.0090x over previous
import numpy as np
import ml_dtypes
import concourse.bass as bass
import concourse.mybir as mybir
from concourse.bass_utils import run_bass_kernel_spmd

F32 = mybir.dt.float32
BF16 = mybir.dt.bfloat16
AF = mybir.ActivationFunctionType
ALU = mybir.AluOpType
AX = mybir.AxisListType

H = 1024
NH = 16
NC = 8
TS = 122
WS = TS + 6
RW = 494
QW = 491
SCALE = 1.0 / 8.0

OWN = 4096
SHARD = OWN + 6
NT = 34
NP = 34


def set_geometry(own):
    global OWN, SHARD, NT, NP
    OWN = own
    SHARD = own + 6
    NT = (OWN + TS - 1) // TS
    NP = (SHARD + TS - 1) // TS


def tsz_of(i):
    return TS if i < NT - 1 else OWN - TS * (NT - 1)


def psz_of(p):
    return TS if p < NP - 1 else SHARD - TS * (NP - 1)


def wsz_of(i):
    return min(WS, SHARD - TS * i)


def build_program():
    nc = bass.Bass("TRN2", target_bir_lowering=False)
    x_h = nc.dram_tensor("x", [SHARD, H], F32, kind="ExternalInput").ap()
    wq_h = nc.dram_tensor("wq", [H, H], F32, kind="ExternalInput").ap()
    wk_h = nc.dram_tensor("wk", [H, H], F32, kind="ExternalInput").ap()
    wv_h = nc.dram_tensor("wv", [H, H], F32, kind="ExternalInput").ap()
    wo_h = nc.dram_tensor("wo", [H, H], F32, kind="ExternalInput").ap()
    wf_h = nc.dram_tensor("wf", [H, H], F32, kind="ExternalInput").ap()
    bq_h = nc.dram_tensor("bq", [4, 256], F32, kind="ExternalInput").ap()
    bv_h = nc.dram_tensor("bv", [128, 8], F32, kind="ExternalInput").ap()
    bo_h = nc.dram_tensor("bo", [1, H], F32, kind="ExternalInput").ap()
    mk_h = nc.dram_tensor("mk", [128, 3 * TS], BF16, kind="ExternalInput").ap()
    i4_h = nc.dram_tensor("ind4", [4, 488], BF16, kind="ExternalInput").ap()
    id_h = nc.dram_tensor("idm", [128, 128], BF16, kind="ExternalInput").ap()
    out_h = nc.dram_tensor("out", [OWN, H], F32, kind="ExternalOutput").ap()

    prog = {e: [] for e in ("sync", "gpsimd", "tensor", "vector", "scalar")}
    cnt = {"t": 0, "v": 0, "s": 0, "g": 0, "lx": 0, "lr": 0, "w1": 0, "w2": 0, "st": 0, "wg": 0, "w3": 0}
    evt = {}
    last_wait = {}

    def op(eng, f):
        last_wait[eng] = {}
        prog[eng].append(("op", f))

    def inc(eng, sem, f, key=None, n=1):
        cnt[sem] += n
        if key is not None:
            evt[key] = cnt[sem]
        last_wait[eng] = {}
        prog[eng].append(("inc", f, sem, n))

    def wait(eng, sem, val):
        if not val or val <= 0:
            return
        pend = last_wait.setdefault(eng, {})
        if pend.get(sem, 0) >= val:
            return
        pend[sem] = val
        prog[eng].append(("wait", sem, val))

    missed_waits = []

    def wait_ev(eng, sem, key):
        if key in evt:
            wait(eng, sem, evt[key])
        else:
            missed_waits.append(key)

    def dma(eng, chain, f, key=None):
        wait(eng, chain, cnt[chain])
        inc(eng, chain, f, key=key, n=16)

    def drn(eng):
        last_wait[eng] = {}
        prog[eng].append(("drain",))

    import contextlib
    with contextlib.ExitStack() as _st:
        wq_b = _st.enter_context(nc.sbuf_tensor("wq_b", [128, 8 * H], BF16))
        wk_b = _st.enter_context(nc.sbuf_tensor("wk_b", [128, 8 * H], BF16))
        wv_b = _st.enter_context(nc.sbuf_tensor("wv_b", [128, 8 * H], BF16))
        wo_b = _st.enter_context(nc.sbuf_tensor("wo_b", [128, 8 * H], BF16))
        wfT_b = _st.enter_context(nc.sbuf_tensor("wfT_b", [128, 8 * H], BF16))
        xnt = _st.enter_context(nc.sbuf_tensor("xnt", [128, 8 * RW], BF16))
        ktr = _st.enter_context(nc.sbuf_tensor("ktr", [128, 8 * RW], BF16))
        qme = _st.enter_context(nc.sbuf_tensor("qme", [128, 8 * QW], BF16))
        qmo = _st.enter_context(nc.sbuf_tensor("qmo", [128, 8 * QW], BF16))
        v_sb = _st.enter_context(nc.sbuf_tensor("v_sb", [128, 2 * NH * 65], BF16))
        ffn_sb = _st.enter_context(nc.sbuf_tensor("ffn_sb", [128, 2 * H], F32))
        ones128b = _st.enter_context(nc.sbuf_tensor("ones128b", [128, 1], BF16))
        e_buf = _st.enter_context(nc.sbuf_tensor("e_buf", [128, NH * TS], BF16))
        au_buf = _st.enter_context(nc.sbuf_tensor("au_buf", [128, NH * TS], BF16))
        attok = _st.enter_context(nc.sbuf_tensor("attok", [128, 2 * H], BF16))
        atT = _st.enter_context(nc.sbuf_tensor("atT", [128, 2 * 8 * TS], BF16))
        xl = _st.enter_context(nc.sbuf_tensor("xl", [128, 3 * H], F32))
        xr = _st.enter_context(nc.sbuf_tensor("xr", [128, 2 * H], F32))
        osb = _st.enter_context(nc.sbuf_tensor("osb", [128, 2 * H], F32))
        xn_sb = _st.enter_context(nc.sbuf_tensor("xn_sb", [128, 2 * H], BF16))
        scr2 = _st.enter_context(nc.sbuf_tensor("scr2", [128, H], BF16))
        gsc = _st.enter_context(nc.sbuf_tensor("gsc", [128, H], F32))
        wstage = _st.enter_context(nc.sbuf_tensor("wstage", [128, 4 * H], F32))
        wtern = _st.enter_context(nc.sbuf_tensor("wtern", [128, 2 * H], BF16))
        mk_sb = _st.enter_context(nc.sbuf_tensor("mk_sb", [128, 3 * TS], BF16))
        id_sb = _st.enter_context(nc.sbuf_tensor("id_sb", [128, 128], BF16))
        brow = _st.enter_context(nc.sbuf_tensor("brow", [1, H], BF16))
        bo_sb = _st.enter_context(nc.sbuf_tensor("bo_sb", [1, H], F32))
        bq_sb = _st.enter_context(nc.sbuf_tensor("bq_sb", [4, 256], F32))
        bqc_b = _st.enter_context(nc.sbuf_tensor("bqc_b", [4, 256], BF16))
        ind4_sb = _st.enter_context(nc.sbuf_tensor("ind4_sb", [4, 488], BF16))
        bv_sb = _st.enter_context(nc.sbuf_tensor("bv_sb", [128, 8], F32))
        bv_b = _st.enter_context(nc.sbuf_tensor("bv_b", [128, 8], BF16))
        ones_c = _st.enter_context(nc.sbuf_tensor("ones_c", [1, TS], BF16))
        ones128 = _st.enter_context(nc.sbuf_tensor("ones128", [128, 1], F32))
        onesrow = _st.enter_context(nc.sbuf_tensor("onesrow", [1, 128], F32))
        stats = _st.enter_context(nc.sbuf_tensor("stats", [128, 16], F32))
        recip = _st.enter_context(nc.sbuf_tensor("recip", [128, 16], F32))
        absacc = _st.enter_context(nc.sbuf_tensor("absacc", [128, 8], F32))
        sc_sc = _st.enter_context(nc.sbuf_tensor("sc_sc", [128, 4], F32))
        cst = _st.enter_context(nc.sbuf_tensor("cst", [128, 4], F32))
        p_tx = _st.enter_context(nc.psum_tensor("p_tx", [128, 1024], BF16))
        p_ta = _st.enter_context(nc.psum_tensor("p_ta", [128, 1024], BF16))
        p_qk = _st.enter_context(nc.psum_tensor("p_qk", [128, 1024], F32))
        p_of = _st.enter_context(nc.psum_tensor("p_of", [128, 1024], F32))
        p_sc = _st.enter_context(nc.psum_tensor("p_sc", [128, 1024], F32))
        S_T = _st.enter_context(nc.semaphore("s_t"))
        S_V = _st.enter_context(nc.semaphore("s_v"))
        S_S = _st.enter_context(nc.semaphore("s_s"))
        S_LX = _st.enter_context(nc.semaphore("s_lx"))
        S_LR = _st.enter_context(nc.semaphore("s_lr"))
        S_W1 = _st.enter_context(nc.semaphore("s_w1"))
        S_W2 = _st.enter_context(nc.semaphore("s_w2"))
        S_ST = _st.enter_context(nc.semaphore("s_st"))
        S_G = _st.enter_context(nc.semaphore("s_g"))
        S_WG = _st.enter_context(nc.semaphore("s_wg"))
        S_W3 = _st.enter_context(nc.semaphore("s_w3"))
        block = _st.enter_context(nc.Block())
        sems = {"t": S_T, "v": S_V, "s": S_S, "g": S_G, "lx": S_LX, "lr": S_LR, "w1": S_W1, "w2": S_W2, "st": S_ST, "wg": S_WG, "w3": S_W3}

        def xnt_ap():
            return xnt[:, :].rearrange("p (c n) -> p c n", c=8, n=RW)

        def ktr_ap():
            return ktr[:, :].rearrange("p (c n) -> p c n", c=8, n=RW)

        def qap(t):
            return t[:, :].rearrange("p (c n) -> p c n", c=8, n=QW)

        const_chains = ("w1", "wg", "w2", "w3", "lr", "lx")
        ld_pairs = []
        for ci, (nm, dst, src) in enumerate((("mk", mk_sb[:, :], mk_h[:, :]),
                                             ("id", id_sb[:, :], id_h[:, :]),
                                             ("bq", bq_sb[:, :], bq_h[:, :]),
                                             ("bv", bv_sb[:, :], bv_h[:, :]),
                                             ("bo", bo_sb[:, :], bo_h[:, :]),
                                             ("i4", ind4_sb[:, :], i4_h[:, :]))):
            ch = const_chains[ci]
            dma("sync", ch, (lambda d, s: (lambda e: e.dma_start(d, s)))(dst, src))
            ld_pairs.append((ch, cnt[ch]))

        for (ch, v0) in ld_pairs:
            wait("vector", ch, v0)
        op("vector", lambda e: e.memset(ones_c[0:1, :], 1.0))
        op("vector", lambda e: e.memset(ones128[:, :], 1.0))
        op("vector", lambda e: e.memset(onesrow[0:1, :], 1.0))
        op("vector", lambda e: e.memset(qme[64:128, :], 0.0))
        op("vector", lambda e: e.memset(qmo[0:64, :], 0.0))
        op("vector", lambda e: e.memset(ones128b[:, :], 1.0))
        op("vector", lambda e: e.memset(
            v_sb[:, :].rearrange("p (s h c) -> p s h c", s=2, h=NH, c=65)[:, :, :, 64:65], 1.0))
        op("vector", lambda e: e.memset(cst[:, 0:1], 1e-5))
        op("vector", lambda e: e.memset(cst[:, 1:2], -0.5))
        op("vector", lambda e: e.memset(cst[:, 2:3], 0.125))
        inc("vector", "v", lambda e: e.memset(stats[:, :], 0.0), key=("const",))
        for (ch, v0) in ld_pairs:
            wait("vector", ch, v0)
        inc("vector", "v", lambda e: e.tensor_copy(bv_b[:, :], bv_sb[:, :]), key=("bvb",))
        inc("vector", "v", lambda e: e.tensor_copy(bqc_b[:, :], bq_sb[:, :]), key=("bqb",))

        cast_done = []

        def emit_wcasts():
            cast_jobs = []
            for (srch, dstb) in ((wq_h, wq_b), (wk_h, wk_b), (wv_h, wv_b), (wo_h, wo_b)):
                for k in range(8):
                    cast_jobs.append((srch, dstb, k))
            chains4 = ("w1", "wg", "w2", "w3")
            cjob_done = []
            for j, (srch, dstb, k) in enumerate(cast_jobs):
                slot = j % 4
                ch = chains4[j % 4]
                if j >= 4:
                    pk, peng = cjob_done[j - 4]
                    wait("sync", {"vector": "v", "scalar": "s"}[peng], evt[pk])
                else:
                    wait("sync", "v", evt[("wtern", 4 + j)])
                dma("sync", ch,
                    (lambda sl, sh, kk: (lambda e: e.dma_start(
                        wstage[:, sl * H:(sl + 1) * H],
                        sh[128 * kk:128 * kk + 128, :])))(slot, srch, k),
                    key=("cwld", j))
                eng = "scalar" if (j % 2 == 0) else "vector"
                wait(eng, ch, evt[("cwld", j)])
                if eng == "scalar":
                    inc(eng, "s", (lambda sl, db, kk: (lambda e: e.copy(
                        db[:, kk * H:(kk + 1) * H], wstage[:, sl * H:(sl + 1) * H])))(slot, dstb, k),
                        key=("wcast", j))
                else:
                    inc(eng, "v", (lambda sl, db, kk: (lambda e: e.tensor_copy(
                        db[:, kk * H:(kk + 1) * H], wstage[:, sl * H:(sl + 1) * H])))(slot, dstb, k),
                        key=("wcast", j))
                cjob_done.append((("wcast", j), eng))
                cast_done.append((("wcast", j), eng))

        def emit_wf_prep():
            wjob_done = []
            jobs = [("wfabs", k) for k in range(4)] + [("wfp2", k) for k in range(4)]
            for j, (kind, k) in enumerate(jobs):
                slot = j % 2
                if j >= 2:
                    pk = wjob_done[j - 2]
                    wait("sync", "v", evt[pk])
                wchain = "w2" if (j % 2 == 0) else "w3"
                dma("sync", wchain,
                    (lambda sl, kk: (lambda e: e.dma_start(
                        wstage[:, 2 * sl * H:(2 * sl + 2) * H].rearrange("p (c n) -> p c n", c=2, n=H),
                        wf_h[256 * kk:256 * kk + 256, :].rearrange("(c p) n -> p c n", c=2, p=128))))(slot, k),
                    key=("wld", j))
                if kind == "wfabs":
                    wait("vector", wchain, evt[("wld", j)])
                    inc("vector", "v", (lambda sl, kk: (lambda e: e.tensor_reduce(
                        absacc[:, 2 * kk:2 * kk + 2],
                        wstage[:, 2 * sl * H:(2 * sl + 2) * H].rearrange("p (c n) -> p c n", c=2, n=H),
                        AX.X, ALU.add,
                        apply_absolute_value=True)))(slot, k), key=("wabs", j))
                    wjob_done.append(("wabs", j))
                    if k == 3:
                        drn("vector")
                        inc("vector", "v", lambda e: e.tensor_reduce(
                            stats[:, 15:16], absacc[:, 0:8], AX.X, ALU.add), key=("wabs_c",))
                else:
                    if k == 0:
                        wait("tensor", "v", evt[("wabs_c",)])
                        inc("tensor", "t", lambda e: e.matmul(
                            p_of[0:1, 8:9], ones128[:, 0:1], stats[:, 15:16], start=True, stop=True),
                            key=("abs_mm",))
                        wait("vector", "t", evt[("abs_mm",)])
                        inc("vector", "v", lambda e: e.tensor_scalar(
                            stats[0:1, 14:15], p_of[0:1, 8:9], 1.0 / (H * H), 1e-5,
                            ALU.mult, ALU.max), key=("scale1",))
                        drn("vector")
                        wait("tensor", "v", evt[("scale1",)])
                        inc("tensor", "t", lambda e: e.matmul(
                            p_of[0:128, 12:13], onesrow[0:1, 0:128], stats[0:1, 14:15],
                            start=True, stop=True), key=("scale_bc",))
                        wait("vector", "t", evt[("scale_bc",)])
                        inc("vector", "v", lambda e: e.tensor_scalar_mul(
                            sc_sc[:, 0:1], p_of[:, 12:13], 1.0), key=("scl0",))
                        inc("vector", "v", lambda e: e.tensor_scalar_mul(
                            sc_sc[:, 1:2], p_of[:, 12:13], 0.5), key=("scl1",))
                        inc("vector", "v", lambda e: e.tensor_scalar_mul(
                            sc_sc[:, 2:3], p_of[:, 12:13], -0.5), key=("scl2",))
                        inc("vector", "v", lambda e: e.tensor_scalar_mul(
                            sc_sc[:, 3:4], p_of[:, 12:13], -1.0), key=("scl3",))
                    wait("vector", wchain, evt[("wld", j)])
                    for kk2 in (("scl0",), ("scl1",), ("scl2",), ("scl3",)):
                        wait_ev("vector", "v", kk2)
                    for sub in range(2):
                        c = 2 * k + sub
                        wsl = c % 2
                        wait_ev("vector", "v", ("wtern", c - 2))
                        wait_ev("vector", "t", ("wfT_t", c - 2))
                        drn("vector")
                        op("vector", (lambda s2, sb: (lambda e: e.tensor_single_scalar(
                            gsc[:, 0:H], wstage[:, (2 * s2 + sb) * H:(2 * s2 + sb + 1) * H],
                            sc_sc[:, 1:2], ALU.is_ge)))(slot, sub))
                        drn("vector")
                        op("vector", (lambda s2, sb: (lambda e: e.scalar_tensor_tensor(
                            gsc[:, 0:H], wstage[:, (2 * s2 + sb) * H:(2 * s2 + sb + 1) * H],
                            sc_sc[:, 2:3], gsc[:, 0:H],
                            ALU.is_le, ALU.subtract)))(slot, sub))
                        drn("vector")
                        inc("vector", "v", (lambda w2: (lambda e: e.tensor_scalar_mul(
                            wtern[:, w2 * H:(w2 + 1) * H], gsc[:, 0:H], sc_sc[:, 3:4])))(wsl),
                            key=("wtern", c))
                        wait("tensor", "v", evt[("wtern", c)])
                        for (ch0, v0) in ld_pairs:
                            wait("tensor", ch0, v0)
                        if c > 0:
                            wait_ev("tensor", "s", ("wfT_ev", c - 1))
                        for ic in range(8):
                            f = (lambda ic2, w2: (lambda e: e.transpose(
                                p_ta[0:128, 128 * ic2:128 * ic2 + 128],
                                wtern[0:128, w2 * H + 128 * ic2:w2 * H + 128 * ic2 + 128],
                                id_sb[:, :])))(ic, wsl)
                            if ic < 7:
                                op("tensor", f)
                            else:
                                inc("tensor", "t", f, key=("wfT_t", c))
                        wait("scalar", "t", evt[("wfT_t", c)])
                        inc("scalar", "s", (lambda kk: (lambda e: e.copy(
                            wfT_b[:, :].rearrange("p (c n) -> p c n", c=8, n=H)[:, :, 128 * kk:128 * kk + 128],
                            p_ta[:, :].rearrange("p (c n) -> p c n", c=8, n=128))))(c),
                            key=("wfT_ev", c))
                    wjob_done.append(("wtern", 2 * k + 1))

        stash_P = {}
        stash_sc2 = {}
        stash_mid = {}

        def stage_XLD(p):
            if p >= 3:
                wait_ev("sync", "s", ("xn", p - 3))
                wait_ev("sync", "v", ("rsum", p - 3))
            dma("sync", "lx", (lambda pp, s2: (lambda e: e.dma_start(
                xl[0:psz_of(pp), s2 * H:(s2 + 1) * H],
                x_h[TS * pp:TS * pp + psz_of(pp), :])))(p, p % 3),
                key=("xld", p))

        def stage_LN_sq(p):
            wait("scalar", "lx", evt[("xld", p)])
            wait_ev("scalar", "v", ("const",))
            sqc = 8 + (p % 2)
            if p >= 2:
                wait_ev("scalar", "v", ("stats2", p - 2))
                wait_ev("scalar", "v", ("sm2", p - 2))
                wait_ev("scalar", "v", ("mbias", p - 2))
                wait_ev("scalar", "s", ("sq", p - 2))
                wait_ev("scalar", "s", ("xn", p - 2))
            if p >= 1:
                wait_ev("scalar", "s", ("sq", p - 1))
            inc("scalar", "s", (lambda pp, s2, c2: (lambda e: e.activation(
                scr2[0:psz_of(pp), 0:H], xl[0:psz_of(pp), s2 * H:(s2 + 1) * H], AF.Square,
                accum_out=stats[0:psz_of(pp), c2:c2 + 1])))(p, p % 3, sqc),
                key=("sq", p))

        def stage_LN_v(p):
            sc = 4 * (p % 2)
            sqc = 8 + (p % 2)
            wait("vector", "lx", evt[("xld", p)])
            wait_ev("vector", "v", ("const",))
            if p >= 2:
                wait_ev("vector", "s", ("xn", p - 2))
                wait_ev("vector", "v", ("mbias", p - 2))
                wait_ev("vector", "v", ("stats2", p - 2))
            inc("vector", "v", (lambda pp, s2, c: (lambda e: e.reduce_sum(
                stats[0:psz_of(pp), c:c + 1], xl[0:psz_of(pp), s2 * H:(s2 + 1) * H], AX.X)))(p, p % 3, sc),
                key=("rsum", p))
            wait_ev("vector", "s", ("sq", p))
            drn("vector")
            op("vector", (lambda pp, c: (lambda e: e.tensor_scalar_mul(
                stats[0:psz_of(pp), c + 1:c + 2], stats[0:psz_of(pp), c:c + 1], 1.0 / H)))(p, sc))
            inc("vector", "v", (lambda pp, c, c2: (lambda e: e.tensor_scalar_mul(
                stats[0:psz_of(pp), c + 2:c + 3], stats[0:psz_of(pp), c2:c2 + 1], 1.0 / H)))(p, sc, sqc),
                key=("sm2", p))
            drn("vector")
            op("vector", (lambda pp, c: (lambda e: e.tensor_tensor(
                stats[0:psz_of(pp), c + 3:c + 4], stats[0:psz_of(pp), c + 1:c + 2],
                stats[0:psz_of(pp), c + 1:c + 2], ALU.mult)))(p, sc))
            drn("vector")
            inc("vector", "v", (lambda pp, c: (lambda e: e.tensor_sub(
                stats[0:psz_of(pp), c + 2:c + 3], stats[0:psz_of(pp), c + 2:c + 3],
                stats[0:psz_of(pp), c + 3:c + 4])))(p, sc), key=("stats2", p))

        def stage_LN_mb(p):
            sc = 4 * (p % 2)
            wait_ev("vector", "s", ("rstd", p))
            drn("vector")
            inc("vector", "v", (lambda pp, c: (lambda e: e.tensor_scalar(
                stats[0:psz_of(pp), c + 1:c + 2], stats[0:psz_of(pp), c + 1:c + 2],
                stats[0:psz_of(pp), c + 3:c + 4], -1.0, ALU.mult, ALU.mult)))(p, sc),
                key=("mbias", p))

        def stage_LN_s(p):
            sc = 4 * (p % 2)
            wait_ev("scalar", "v", ("stats2", p))
            op("scalar", (lambda pp, c: (lambda e: e.activation(
                stats[0:psz_of(pp), c + 3:c + 4], stats[0:psz_of(pp), c + 2:c + 3], AF.Ln,
                bias=cst[0:psz_of(pp), 0:1])))(p, sc))
            drn("scalar")
            inc("scalar", "s", (lambda pp, c: (lambda e: e.activation(
                stats[0:psz_of(pp), c + 3:c + 4], stats[0:psz_of(pp), c + 3:c + 4], AF.Exp,
                scale=cst[0:psz_of(pp), 1:2])))(p, sc), key=("rstd", p))

        def stage_LN_xn(p):
            sc = 4 * (p % 2)
            sl = p % 2
            wait_ev("scalar", "v", ("mbias", p))
            wait_ev("scalar", "s", ("rstd", p))
            if p >= 2:
                wait_ev("scalar", "t", ("txn", p - 2))
            inc("scalar", "s", (lambda pp, s2, c, sl2: (lambda e: e.activation(
                xn_sb[0:psz_of(pp), sl2 * H:(sl2 + 1) * H], xl[0:psz_of(pp), s2 * H:(s2 + 1) * H],
                AF.Identity, bias=stats[0:psz_of(pp), c + 1:c + 2],
                scale=stats[0:psz_of(pp), c + 3:c + 4])))(p, p % 3, sc, sl),
                key=("xn", p))

        def stage_P_txn(p):
            psz = psz_of(p)
            rc = (TS * p) % 488
            sl = p % 2
            wait_ev("tensor", "s", ("xn", p))
            if p >= 1:
                wait_ev("tensor", "v", ("xnt_ev", p - 1))
            for c in range(8):
                f = (lambda pp, cc, sl2: (lambda e: e.transpose(
                    p_tx[0:128, TS * cc:TS * cc + psz_of(pp)],
                    xn_sb[0:psz_of(pp), sl2 * H + 128 * cc: sl2 * H + 128 * cc + 128],
                    id_sb[0:psz_of(pp), 0:psz_of(pp)])))(p, c, sl)
                if c < 7:
                    op("tensor", f)
                else:
                    inc("tensor", "t", f, key=("txn", p))
            wait_ev("vector", "t", ("txn", p))
            if p >= 4:
                wait_ev("vector", "t", ("ffn", p - 4))
                wait_ev("vector", "t", ("vwin", p - 4))
                wait_ev("vector", "v", ("xnt_ev", p - 4))
                wait_ev("vector", "v", ("xnt_ev2", p - 4))
            inc("vector", "v", (lambda pp, r2: (lambda e: e.tensor_copy(
                xnt_ap()[:, :, r2:r2 + psz_of(pp)],
                p_tx[:, 0:8 * TS].rearrange("p (c n) -> p c n", c=8, n=TS)[:, :, 0:psz_of(pp)])))(p, rc),
                key=("xnt_ev", p))
            if rc == 0:
                inc("vector", "v", (lambda pp: (lambda e: e.tensor_copy(
                    xnt_ap()[:, :, 488:494],
                    p_tx[:, 0:8 * TS].rearrange("p (c n) -> p c n", c=8, n=TS)[:, :, 0:6])))(p),
                    key=("xnt_ev2", p))

        def stage_P(p):
            psz = psz_of(p)
            rc = (TS * p) % 488
            wait_ev("tensor", "v", ("xnt_ev", p))
            if rc == 0:
                wait_ev("tensor", "v", ("xnt_ev2", p))
            [wait("tensor", c0, v0) for (c0, v0) in w_ready_pairs["wq"]]
            [wait("tensor", c0, v0) for (c0, v0) in w_ready_pairs["wk"]]
            if p >= 1:
                wait_ev("tensor", "s", ("kevB", p - 1))
                wait_ev("tensor", "v", ("qev_o", p - 1))
            if p >= 2:
                wait_ev("tensor", "s", ("ffnev", p - 2))
                wait_ev("tensor", "s", ("v65", p - 2))
            if p >= 4:
                wait_ev("tensor", "t", ("sc", p - 4, 1))

            def proj_bank(wb, bank, kkey):
                with_bias = wb is wq_b
                for mc in range(4 * bank, 4 * bank + 4):
                    col = 512 * bank + TS * (mc % 4)
                    for kc in range(8):
                        first = (mc % 4 == 0 and kc == 0)
                        last = (mc % 4 == 3 and kc == 7)
                        f = (lambda pp, m2, k2, c2, r2, wb2, st2, sp2: (lambda e: e.matmul(
                            p_qk[0:128, c2:c2 + psz_of(pp)],
                            wb2[0:128, k2 * H + 128 * m2: k2 * H + 128 * m2 + 128],
                            xnt_ap()[:, k2, r2:r2 + psz_of(pp)],
                            start=st2, stop=sp2)))(p, mc, kc, col, rc, wb,
                                                   first if with_bias else (kc == 0),
                                                   False if with_bias else (kc == 7))
                        if last and not with_bias:
                            inc("tensor", "t", f, key=kkey)
                        else:
                            op("tensor", f)
                if with_bias:
                    if psz == TS:
                        fb = (lambda pp, b2: (lambda e: e.matmul(
                            p_qk[0:128, 512 * b2:512 * b2 + 488],
                            bqc_b[0:4, 128 * b2:128 * b2 + 128],
                            ind4_sb[0:4, 0:488],
                            start=False, stop=True)))(p, bank)
                        inc("tensor", "t", fb, key=kkey)
                    else:
                        for mc4 in range(4):
                            fb = (lambda pp, b2, m4: (lambda e: e.matmul(
                                p_qk[0:128, 512 * b2 + TS * m4:512 * b2 + TS * m4 + psz_of(pp)],
                                bqc_b[0:4, 128 * b2:128 * b2 + 128],
                                ind4_sb[0:4, TS * m4:TS * m4 + psz_of(pp)],
                                start=False, stop=(m4 == 3))))(p, bank, mc4)
                            if mc4 == 3:
                                inc("tensor", "t", fb, key=kkey)
                            else:
                                op("tensor", fb)

            def evac_bank(kind, bank, srckey, outkey):
                wait_ev("scalar", "t", srckey)
                if kind == "q":
                    wait_ev("vector", "t", srckey)
                if p >= 4:
                    wait_ev("scalar", "s", (("qev_e", p - 4) if kind == "q" else ("kevB", p - 4)))
                    if kind == "q":
                        wait_ev("vector", "v", ("qev_o", p - 4))
                    wait_ev("scalar", "t", ("sc", p - 4, 1))
                    if kind == "q":
                        wait_ev("vector", "t", ("sc", p - 4, 1))
                    else:
                        wait_ev("scalar", "t", ("vwin", p - 4))
                dups = [(rc, psz_of(p))] + ([(488, 3 if kind == "q" else 6)] if rc == 0 else [])
                if kind == "q":
                    for di, (q2, w2) in enumerate(dups):
                        f_e = (lambda pp, b2, qq, ww: (lambda e: e.copy(
                            qap(qme)[0:64, 4 * b2:4 * b2 + 4, qq:qq + ww],
                            p_qk[0:64, :].rearrange("p (b n) -> p b n", b=2, n=512)[:, b2, 0:488]
                            .rearrange("p (c n) -> p c n", c=4, n=TS)[:, :, 0:ww])))(p, bank, q2, w2)
                        f_o = (lambda pp, b2, qq, ww: (lambda e: e.tensor_copy(
                            qap(qmo)[64:128, 4 * b2:4 * b2 + 4, qq:qq + ww],
                            p_qk[64:128, :].rearrange("p (b n) -> p b n", b=2, n=512)[:, b2, 0:488]
                            .rearrange("p (c n) -> p c n", c=4, n=TS)[:, :, 0:ww])))(p, bank, q2, w2)
                        last = (di == len(dups) - 1)
                        inc("scalar", "s", f_e, key=(outkey[0], p) if last else None)
                        inc("vector", "v", f_o, key=(outkey[1], p) if last else None)
                else:
                    for di, (r2, w2) in enumerate(dups):
                        f = (lambda pp, b2, rr, ww: (lambda e: e.copy(
                            ktr_ap()[:, 4 * b2:4 * b2 + 4, rr:rr + ww],
                            p_qk[:, :].rearrange("p (b n) -> p b n", b=2, n=512)[:, b2, 0:488]
                            .rearrange("p (c n) -> p c n", c=4, n=TS)[:, :, 0:ww])))(p, bank, r2, w2)
                        inc("scalar", "s", f, key=(outkey, p) if di == len(dups) - 1 else None)

            proj_bank(wq_b, 0, ("qpsA", p))
            if p >= 2:
                wait_ev("tensor", "v", ("recip", p - 2))
                wait_ev("tensor", "v", ("nrm", p - 2))
            proj_bank(wq_b, 1, ("qpsB", p))
            evac_bank("q", 0, ("qpsA", p), ("qev_eA", "qev_oA"))
            stash_P[p] = (proj_bank, evac_bank)

        def stage_P_rest(p):
            proj_bank, evac_bank = stash_P.pop(p)
            wait_ev("tensor", "s", ("qev_eA", p))
            wait_ev("tensor", "v", ("qev_oA", p))
            proj_bank(wk_b, 0, ("kpsA", p))
            evac_bank("q", 1, ("qpsB", p), ("qev_e", "qev_o"))
            evac_bank("k", 0, ("kpsA", p), "kevA")
            wait_ev("tensor", "s", ("qev_e", p))
            wait_ev("tensor", "v", ("qev_o", p))
            proj_bank(wk_b, 1, ("kpsB", p))
            evac_bank("k", 1, ("kpsB", p), "kevB")

        def stage_A_scores(i):
            tsz = tsz_of(i)
            wsz = wsz_of(i)
            rc = (TS * i) % 488
            qrc = (TS * i + 3) % 488
            vt = 0 if i == 0 else (2 if i == NT - 1 else 1)
            vsl = i % 2

            if i >= 2:
                wait_ev("sync", "g", ("out2", i - 2))
            dma("sync", "lr", (lambda ii, s2: (lambda e: e.dma_start(
                xr[0:tsz_of(ii), s2 * H:(s2 + 1) * H],
                x_h[3 + TS * ii: 3 + TS * ii + tsz_of(ii), :])))(i, i % 2),
                key=("xrld", i))

            def sc_half(half):
                if half == 1:
                    wait_ev("tensor", "s", ("exp", i, 0))
                    wait_ev("tensor", "s", ("kevB", min(i + 1, NP - 1)))
                else:
                    if i >= 1:
                        wait_ev("tensor", "v", ("nrm", i - 1))
                    wait_ev("tensor", "s", ("qev_e", min(i + 1, NP - 1)))
                    wait_ev("tensor", "v", ("qev_o", min(i + 1, NP - 1)))
                    wait_ev("tensor", "s", ("kevA", min(i + 1, NP - 1)))
                for hh in range(8):
                    h = half * 8 + hh
                    ch = h // 2
                    qring = qme if (h % 2 == 0) else qmo
                    col = 512 * (hh // 4) + TS * (hh % 4)
                    f = (lambda ii, c2, qr, cl, r2, q2: (lambda e: e.matmul(
                        p_sc[0:wsz_of(ii), cl:cl + tsz_of(ii)],
                        ktr_ap()[:, c2, r2:r2 + wsz_of(ii)],
                        qap(qr)[:, c2, q2:q2 + tsz_of(ii)],
                        start=True, stop=True)))(i, ch, qring, col, rc, qrc)
                    if hh == 7:
                        inc("tensor", "t", f, key=("sc", i, half))
                    else:
                        op("tensor", f)
                wait_ev("scalar", "t", ("sc", i, half))
                if i >= 1:
                    wait_ev("scalar", "v", ("au", i - 1, half))
                    wait_ev("scalar", "s", ("exp", i - 1, half))
                inc("scalar", "s", (lambda ii, hf: (lambda e: e.activation(
                    e_buf[0:wsz_of(ii), 8 * hf * TS:(8 * hf + 8) * TS]
                    .rearrange("p (a b n) -> p a b n", a=2, b=4, n=TS)[:, :, :, 0:tsz_of(ii)],
                    p_sc[0:wsz_of(ii), :].rearrange("p (b n) -> p b n", b=2, n=512)
                    [:, :, 0:488].rearrange("p b (g n) -> p b g n", g=4, n=TS)
                    [:, :, :, 0:tsz_of(ii)],
                    AF.Exp, scale=cst[0:wsz_of(ii), 2:3])))(i, hf=half), key=("exp", i, half))
                wait_ev("vector", "s", ("exp", i, half))
                if i >= 1:
                    wait_ev("vector", "t", ("attnV", i - 1))
                    wait_ev("vector", "v", ("au", i - 1, half))
                inc("vector", "v", (lambda ii, hf, v2: (lambda e: e.tensor_tensor(
                    au_buf[0:wsz_of(ii), :].rearrange("p (h n) -> p h n", h=NH, n=TS)
                    [:, 8 * hf:8 * hf + 8, 0:tsz_of(ii)],
                    e_buf[0:wsz_of(ii), :].rearrange("p (h n) -> p h n", h=NH, n=TS)
                    [:, 8 * hf:8 * hf + 8, 0:tsz_of(ii)],
                    mk_sb[0:wsz_of(ii), TS * v2:TS * v2 + tsz_of(ii)][:, None, :]
                    .broadcast_to([wsz_of(ii), 8, tsz_of(ii)]),
                    ALU.mult)))(i, half, vt), key=("au", i, half))

            sc_half(0)
            stash_mid[i] = True
            [wait("tensor", c0, v0) for (c0, v0) in w_ready_pairs["wv"]]
            if i >= 2:
                wait_ev("tensor", "t", ("attnV", i - 2))
            for jh in range(2):
                if jh == 0:
                    wait_ev("tensor", "s", ("kevA", min(i + 1, NP - 1)))
                    wait_ev("tensor", "v", ("qev_oA", min(i + 1, NP - 1)))
                else:
                    wait_ev("tensor", "s", ("kevB", min(i + 1, NP - 1)))
                    wait_ev("tensor", "v", ("qev_o", min(i + 1, NP - 1)))
                    if i >= 1:
                        wait_ev("tensor", "v", ("recip", i - 1))
                        wait_ev("tensor", "v", ("nrm", i - 1))
                for kc in range(8):
                    f = (lambda ii, k2, j2, r2: (lambda e: e.matmul(
                        p_qk[0:wsz_of(ii), 512 * j2:512 * j2 + 512],
                        xnt_ap()[:, k2, r2:r2 + wsz_of(ii)],
                        wv_b[0:128, k2 * H + 512 * j2: k2 * H + 512 * j2 + 512],
                        start=(k2 == 0), stop=(k2 == 7))))(i, kc, jh, rc)
                    if kc == 7 and jh == 1:
                        inc("tensor", "t", f, key=("vwin", i))
                    else:
                        op("tensor", f)
            wait_ev("scalar", "t", ("vwin", i))
            if i >= 2:
                wait_ev("scalar", "t", ("attnV", i - 2))
                wait_ev("scalar", "s", ("v65", i - 2))
            inc("scalar", "s", (lambda ii, s2: (lambda e: e.copy(
                v_sb[0:wsz_of(ii), s2 * NH * 65:(s2 + 1) * NH * 65]
                .rearrange("p (h c) -> p h c", h=NH, c=65)[:, :, 0:64],
                p_qk[0:wsz_of(ii), 0:H].rearrange("p (h c) -> p h c", h=NH, c=64))))(i, vsl),
                key=("v65", i))
            stash_sc2[i] = sc_half

        def stage_O_tat(i):
            tsz = tsz_of(i)
            osl = i % 2
            wait_ev("tensor", "v", ("nrm", i))
            if i == 0:
                wait_ev("tensor", "s", ("wfT_ev", 7))
            if i >= 1:
                wait_ev("tensor", "s", ("atT_ev", i - 1))
            for c in range(8):
                f = (lambda ii, cc, a2: (lambda e: e.transpose(
                    p_ta[0:128, TS * cc:TS * cc + tsz_of(ii)],
                    attok[0:tsz_of(ii), a2 * H + 128 * cc: a2 * H + 128 * cc + 128],
                    id_sb[0:tsz_of(ii), 0:tsz_of(ii)])))(i, c, osl)
                if c < 7:
                    op("tensor", f)
                else:
                    inc("tensor", "t", f, key=("tat", i))
            wait_ev("scalar", "t", ("tat", i))
            if i >= 2:
                wait_ev("scalar", "t", ("oproj", i - 2))
                wait_ev("scalar", "s", ("atT_ev", i - 2))
            inc("scalar", "s", (lambda ii, a2: (lambda e: e.copy(
                atT[:, :].rearrange("p (s c n) -> p s c n", s=2, c=8, n=TS)[:, a2, :, 0:tsz_of(ii)],
                p_ta[:, 0:8 * TS].rearrange("p (c n) -> p c n", c=8, n=TS)[:, :, 0:tsz_of(ii)])))(i, osl),
                key=("atT_ev", i))

        def stage_A_ffn(i):
            tsz = tsz_of(i)
            qrc = (TS * i + 3) % 488
            wait_ev("tensor", "s", ("v65", i))
            wait_ev("tensor", "s", ("wfT_ev", 7))
            if i >= 1:
                wait_ev("tensor", "v", ("recip", i - 1))
                wait_ev("tensor", "v", ("nrm", i - 1))
            for kc in range(8):
                for jh in range(2):
                    f = (lambda ii, k2, j2, q2: (lambda e: e.matmul(
                        p_qk[0:tsz_of(ii), 512 * j2:512 * j2 + 512],
                        xnt_ap()[:, k2, q2:q2 + tsz_of(ii)],
                        wfT_b[0:128, k2 * H + 512 * j2: k2 * H + 512 * j2 + 512],
                        start=(k2 == 0), stop=(k2 == 7))))(i, kc, jh, qrc)
                    if kc == 7 and jh == 1:
                        inc("tensor", "t", f, key=("ffn", i))
                    else:
                        op("tensor", f)
            wait_ev("scalar", "t", ("ffn", i))
            if i >= 2:
                wait_ev("scalar", "s", ("ffnev", i - 2))
                wait_ev("scalar", "v", ("out1", i - 2))
            inc("scalar", "s", (lambda ii, s2: (lambda e: e.copy(
                ffn_sb[0:tsz_of(ii), s2 * H:(s2 + 1) * H],
                p_qk[0:tsz_of(ii), 0:H])))(i, i % 2),
                key=("ffnev", i))

        def stage_O_proj(i):
            tsz = tsz_of(i)
            osl = i % 2
            if i >= 1:
                wait_ev("tensor", "v", ("out1", i - 1))
            if i == 0:
                wait_ev("tensor", "v", ("scl3",))
            wait_ev("tensor", "v", ("brow",))
            wait_ev("tensor", "s", ("atT_ev", i))
            [wait("tensor", c0, v0) for (c0, v0) in w_ready_pairs["wo"]]
            for jh in range(2):
                op("tensor", (lambda ii, j2: (lambda e: e.matmul(
                    p_of[0:tsz_of(ii), 512 * j2:512 * j2 + 512],
                    ones_c[0:1, 0:tsz_of(ii)], brow[0:1, 512 * j2:512 * j2 + 512],
                    start=True, stop=False)))(i, jh))
            for kc in range(8):
                for jh in range(2):
                    f = (lambda ii, k2, j2, a2: (lambda e: e.matmul(
                        p_of[0:tsz_of(ii), 512 * j2:512 * j2 + 512],
                        atT[:, :].rearrange("p (s c n) -> p s c n", s=2, c=8, n=TS)[:, a2, k2, 0:tsz_of(ii)],
                        wo_b[0:128, k2 * H + 512 * j2: k2 * H + 512 * j2 + 512],
                        start=False, stop=(k2 == 7))))(i, kc, jh, osl)
                    if kc == 7 and jh == 1:
                        inc("tensor", "t", f, key=("oproj", i))
                    else:
                        op("tensor", f)

        def stage_A_attnv(i):
            tsz = tsz_of(i)
            wsz = wsz_of(i)
            vsl = i % 2
            osl = i % 2
            wait_ev("tensor", "v", ("au", i, 0))
            wait_ev("tensor", "v", ("au", i, 1))
            wait_ev("tensor", "s", ("exp", i, 1))
            wait_ev("tensor", "s", ("v65", i))
            wait_ev("tensor", "s", ("ffnev", i))
            for h in range(NH):
                if h < 7:
                    c0 = 65 * h
                elif h < 14:
                    c0 = 512 + 65 * (h - 7)
                else:
                    c0 = 894 + 65 * (h - 14)
                dst = (p_sc if h < 14 else p_qk)[0:tsz_of(i), c0:c0 + 65]
                f = (lambda ii, h2, v2, d2: (lambda e: e.matmul(
                    d2,
                    au_buf[0:wsz_of(ii), TS * h2:TS * h2 + tsz_of(ii)],
                    v_sb[0:wsz_of(ii), v2 * NH * 65 + 65 * h2: v2 * NH * 65 + 65 * h2 + 65],
                    start=True, stop=True)))(i, h, vsl, dst)
                if h == NH - 1:
                    inc("tensor", "t", f, key=("attnV", i))
                else:
                    op("tensor", f)
            return

        def stage_Av_recip(i):
            wait_ev("vector", "t", ("attnV", i))
            if i >= 2:
                wait_ev("vector", "v", ("recip", i - 2))
            op("vector", (lambda ii: (lambda e: e.reciprocal(
                recip[0:tsz_of(ii), 0:7],
                p_sc[0:tsz_of(ii), 0:455].rearrange("p (h c) -> p h c", h=7, c=65)[:, :, 64])))(i))
            op("vector", (lambda ii: (lambda e: e.reciprocal(
                recip[0:tsz_of(ii), 7:14],
                p_sc[0:tsz_of(ii), 512:967].rearrange("p (h c) -> p h c", h=7, c=65)[:, :, 64])))(i))
            inc("vector", "v", (lambda ii: (lambda e: e.reciprocal(
                recip[0:tsz_of(ii), 14:16],
                p_qk[0:tsz_of(ii), 894:1024].rearrange("p (h c) -> p h c", h=2, c=65)[:, :, 64])))(i),
                key=("recip", i))

        def stage_Av_nrm(i):
            osl = i % 2
            if i >= 2:
                wait_ev("vector", "t", ("tat", i - 2))
                wait_ev("vector", "v", ("nrm", i - 2))
            drn("vector")
            op("vector", (lambda ii, a2: (lambda e: e.tensor_tensor(
                attok[0:tsz_of(ii), a2 * H:a2 * H + 448].rearrange("p (h c) -> p h c", h=7, c=64),
                p_sc[0:tsz_of(ii), 0:455].rearrange("p (h c) -> p h c", h=7, c=65)[:, :, 0:64],
                recip[0:tsz_of(ii), 0:7][:, :, None].broadcast_to([tsz_of(ii), 7, 64]),
                ALU.mult)))(i, osl))
            op("vector", (lambda ii, a2: (lambda e: e.tensor_tensor(
                attok[0:tsz_of(ii), a2 * H + 448:a2 * H + 896].rearrange("p (h c) -> p h c", h=7, c=64),
                p_sc[0:tsz_of(ii), 512:967].rearrange("p (h c) -> p h c", h=7, c=65)[:, :, 0:64],
                recip[0:tsz_of(ii), 7:14][:, :, None].broadcast_to([tsz_of(ii), 7, 64]),
                ALU.mult)))(i, osl))
            inc("vector", "v", (lambda ii, a2: (lambda e: e.tensor_tensor(
                attok[0:tsz_of(ii), a2 * H + 896:a2 * H + 1024].rearrange("p (h c) -> p h c", h=2, c=64),
                p_qk[0:tsz_of(ii), 894:1024].rearrange("p (h c) -> p h c", h=2, c=65)[:, :, 0:64],
                recip[0:tsz_of(ii), 14:16][:, :, None].broadcast_to([tsz_of(ii), 2, 64]),
                ALU.mult)))(i, osl), key=("nrm", i))

        def stage_O_res(i):
            tsz = tsz_of(i)
            osl = i % 2
            wait_ev("vector", "t", ("oproj", i))
            wait_ev("vector", "s", ("ffnev", i))
            if i >= 2:
                wait("vector", "st", 16 * (i - 1))
                wait_ev("vector", "g", ("out2", i - 2))
            inc("vector", "v", (lambda ii, o2: (lambda e: e.tensor_add(
                osb[0:tsz_of(ii), o2 * H:(o2 + 1) * H],
                p_of[0:tsz_of(ii), 0:H],
                ffn_sb[0:tsz_of(ii), (ii % 2) * H:((ii % 2) + 1) * H])))(i, osl), key=("out1", i))
            if i == NT - 1:
                wait("vector", "lr", evt[("xrld", i)])
                drn("vector")
                inc("vector", "g", (lambda ii, s2, o2: (lambda e: e.tensor_add(
                    osb[0:tsz_of(ii), o2 * H:(o2 + 1) * H],
                    osb[0:tsz_of(ii), o2 * H:(o2 + 1) * H],
                    xr[0:tsz_of(ii), s2 * H:(s2 + 1) * H])))(i, i % 2, osl), key=("out2", i))
                wait_ev("gpsimd", "v", ("out1", i))
            else:
                wait_ev("gpsimd", "v", ("out1", i))
                wait("gpsimd", "lr", evt[("xrld", i)])
                inc("gpsimd", "g", (lambda ii, s2, o2: (lambda e: e.tensor_add(
                    osb[0:tsz_of(ii), o2 * H:(o2 + 1) * H],
                    osb[0:tsz_of(ii), o2 * H:(o2 + 1) * H],
                    xr[0:tsz_of(ii), s2 * H:(s2 + 1) * H])))(i, i % 2, osl), key=("out2", i))
            wait_ev("gpsimd", "g", ("out2", i))
            dma("gpsimd", "st", (lambda ii, o2: (lambda e: e.dma_start(
                out_h[TS * ii:TS * ii + tsz_of(ii), :],
                osb[0:tsz_of(ii), o2 * H:(o2 + 1) * H])))(i, osl))


        for p0 in range(min(3, NP)):
            stage_XLD(p0)
        for p0 in range(min(2, NP)):
            stage_LN_sq(p0)
            stage_LN_v(p0)
            stage_LN_s(p0)
            stage_LN_mb(p0)
            stage_LN_xn(p0)
        stage_P_txn(0)

        emit_wf_prep()
        emit_wcasts()
        w_ready_pairs = {}
        for nm, hi in (("wq", 7), ("wk", 15), ("wv", 23), ("wo", 31)):
            w_ready_pairs[nm] = (("s", evt[("wcast", hi - 1)]), ("v", evt[("wcast", hi)]))

        def emit_brow():
            [wait("tensor", c0, v0) for (c0, v0) in w_ready_pairs["wo"]]
            wait("tensor", "v", evt[("bvb",)])
            wait("tensor", "v", evt[("scl3",)])
            for k in range(8):
                for jh in range(2):
                    f = (lambda kk, j2: (lambda e: e.matmul(
                        p_of[0:1, 512 * j2:512 * j2 + 512], bv_b[:, kk:kk + 1],
                        wo_b[:, kk * H + 512 * j2: kk * H + 512 * j2 + 512],
                        start=(kk == 0), stop=(kk == 7))))(k, jh)
                    if k == 7 and jh == 1:
                        inc("tensor", "t", f, key=("bvwo",))
                    else:
                        op("tensor", f)
            wait("vector", "t", evt[("bvwo",)])
            inc("vector", "v", lambda e: e.tensor_add(brow[0:1, :], p_of[0:1, 0:H], bo_sb[0:1, :]),
                key=("brow",))




        for s in range(NP + 3):
            ia = s - 1
            io = s - 2
            ir = s - 3
            if s + 3 < NP:
                stage_XLD(s + 3)
            if 0 <= io < NT:
                stage_Av_recip(io)
                stage_Av_nrm(io)
            if s < NP:
                stage_P(s)
            if s < NP:
                stage_P_rest(s)
            if 0 <= ir < NT:
                stage_O_res(ir)
            if 0 <= ia < NT:
                stage_A_scores(ia)
            if s + 1 < NP:
                stage_P_txn(s + 1)
            if 0 <= io < NT:
                stage_O_tat(io)
            if s + 2 < NP:
                stage_LN_sq(s + 2)
                stage_LN_v(s + 2)
                stage_LN_s(s + 2)
                stage_LN_mb(s + 2)
                stage_LN_xn(s + 2)
            if 0 <= ia < NT:
                stash_sc2.pop(ia)(1)
                stage_A_ffn(ia)
            if s == 2:
                emit_brow()
            if 0 <= io < NT:
                stage_O_proj(io)
            if 0 <= ia < NT:
                stage_A_attnv(ia)

        wait("gpsimd", "st", 16 * NT)
        late = [k for k in missed_waits if k in evt]
        assert not late, f"wait_ev skipped keys registered later (emission-order bug): {sorted(set(late))[:8]}"


        def replay(e, entries):
            for ent in entries:
                if ent[0] == "op":
                    ent[1](e)
                elif ent[0] == "inc":
                    ent[1](e).then_inc(sems[ent[2]], ent[3])
                elif ent[0] == "drain":
                    e.drain()
                else:
                    e.wait_ge(sems[ent[1]], ent[2])

        @block.sync
        def _(sync):
            replay(sync, prog["sync"])

        @block.gpsimd
        def _(g):
            replay(g, prog["gpsimd"])

        @block.tensor
        def _(t):
            replay(t, prog["tensor"])

        @block.vector
        def _(v):
            replay(v, prog["vector"])

        @block.scalar
        def _(s):
            replay(s, prog["scalar"])

    return nc


def make_masks(core, n_cores=NC):
    jj = np.arange(128)[:, None]
    tl = np.arange(TS)[None, :]
    band = ((jj - tl >= 0) & (jj - tl <= 6)).astype(np.float32)
    m = np.stack([band.copy(), band.copy(), band.copy()])
    half = n_cores // 2
    if core % half == 0:
        m[0][:3, :] = 0.0
    tszl = tsz_of(NT - 1)
    wszl = wsz_of(NT - 1)
    m2 = band.copy()
    m2[wszl:, :] = 0.0
    m2[:, tszl:] = 0.0
    if core % half == half - 1:
        m2[tszl + 3:, :] = 0.0
    m[2] = m2
    return m.astype(ml_dtypes.bfloat16)


def kernel(L0_attn_hidden_states, L0_attn_w_q, L0_attn_b_q, L0_attn_w_k, L0_attn_b_k,
           L0_attn_w_v, L0_attn_b_v, L0_attn_w_o, L0_attn_b_o, L0_ffn_weight):
    x = np.ascontiguousarray(np.asarray(L0_attn_hidden_states, dtype=np.float32).reshape(-1, H))
    n_tok = x.shape[0]
    assert n_tok == NC * OWN
    xp = np.concatenate([np.zeros((3, H), np.float32), x, np.zeros((3, H), np.float32)], axis=0)

    wq = np.ascontiguousarray(L0_attn_w_q, dtype=np.float32)
    wk = np.ascontiguousarray(L0_attn_w_k, dtype=np.float32)
    wv = np.ascontiguousarray(L0_attn_w_v, dtype=np.float32)
    wo = np.ascontiguousarray(L0_attn_w_o, dtype=np.float32)
    wf = np.ascontiguousarray(L0_ffn_weight, dtype=np.float32)
    bqr = np.asarray(L0_attn_b_q, np.float32).reshape(8, 128)
    bq = np.ascontiguousarray(np.concatenate([bqr[0:4], bqr[4:8]], axis=1))
    ind4 = (np.arange(488)[None, :] // TS == np.arange(4)[:, None]).astype(ml_dtypes.bfloat16)
    bv = np.ascontiguousarray(np.asarray(L0_attn_b_v, np.float32).reshape(8, 128).T)
    bo = np.ascontiguousarray(np.asarray(L0_attn_b_o, np.float32).reshape(1, H))
    idm = np.eye(128).astype(ml_dtypes.bfloat16)

    nc = build_program()
    in_maps = []
    for c in range(NC):
        shard = np.ascontiguousarray(xp[c * OWN: c * OWN + SHARD])
        in_maps.append({
            "x": shard, "wq": wq, "wk": wk, "wv": wv, "wo": wo, "wf": wf,
            "bq": bq, "bv": bv, "bo": bo, "ind4": ind4,
            "mk": np.ascontiguousarray(make_masks(c).transpose(1, 0, 2).reshape(128, 3 * TS)),
            "idm": idm,
        })
    res = run_bass_kernel_spmd(nc, in_maps, core_ids=list(range(NC)))
    out = np.concatenate([r["out"] for r in res.results], axis=0)
    return out.reshape(2, n_tok // 2, H).astype(np.float32)
